# version 19
# baseline (speedup 1.0000x reference)
import numpy as np
import ml_dtypes
from contextlib import ExitStack

import concourse.bass as bass
import concourse.mybir as mybir
from concourse.bass_utils import run_bass_kernel_spmd

F32 = mybir.dt.float32
BF16 = mybir.dt.bfloat16
I8 = mybir.dt.int8
U8 = mybir.dt.uint8
AF = mybir.ActivationFunctionType
ALU = mybir.AluOpType
AX = mybir.AxisListType

NCORE = 8
T = 1024
NT = 8
E = 2048
EC = 16
H = 16
D = 128
M = 1024
MC = 8
II = 8192
IC = 64
HQ = 4

EPS_LN = 1e-5
EPS_RMS = 1e-6

ARENA_BYTES = 188416


class Sem:
    def __init__(self, h):
        self.h = h
        self.n = 0

    def tick(self, k=1):
        self.n += k
        return self.n


def _build_nc():
    nc = bass.Bass("TRN2", target_bir_lowering=False)

    x_d = nc.dram_tensor("x", [NT, 128, E], F32, kind="ExternalInput").ap()
    win_d = nc.dram_tensor("win", [H, 128, E], F32, kind="ExternalInput").ap()
    wqT_d = nc.dram_tensor("wqT", [128, 128], F32, kind="ExternalInput").ap()
    wkT_d = nc.dram_tensor("wkT", [128, 128], F32, kind="ExternalInput").ap()
    wvT_d = nc.dram_tensor("wvT", [128, 128], F32, kind="ExternalInput").ap()
    patT_d = nc.dram_tensor("patT", [H, 128, M], F32, kind="ExternalInput").ap()
    nqw_d = nc.dram_tensor("nqw", [128, EC], F32, kind="ExternalInput").ap()
    wr_d = nc.dram_tensor("wr", [128, E], F32, kind="ExternalInput").ap()
    sd_d = nc.dram_tensor("sd", [128, E], F32, kind="ExternalInput").ap()
    sgc_d = nc.dram_tensor("sgc", [128, IC], F32, kind="ExternalInput").ap()
    suc_d = nc.dram_tensor("suc", [128, IC], F32, kind="ExternalInput").ap()
    beta_d = nc.dram_tensor("beta", [1, 1], F32, kind="ExternalInput").ap()
    wg_d = nc.dram_tensor("wg", [IC, 128, E], I8, kind="ExternalInput").ap()
    wu_d = nc.dram_tensor("wu", [IC, 128, E], I8, kind="ExternalInput").ap()
    wd_d = nc.dram_tensor("wd", [HQ, 16, 128, E], I8, kind="ExternalInput").ap()
    id_d = nc.dram_tensor("ident", [128, 128], BF16, kind="ExternalInput").ap()
    om_d = nc.dram_tensor("onemat", [128, 128], BF16, kind="ExternalInput").ap()
    or_d = nc.dram_tensor("onesrow", [1, 128], BF16, kind="ExternalInput").ap()
    of_d = nc.dram_tensor("onef", [1, 128], F32, kind="ExternalInput").ap()

    out_d = nc.dram_tensor("out", [NT, 128, E], F32, kind="ExternalOutput").ap()
    x1sp_d = nc.dram_tensor("x1sp", [NT, 128, E], F32, kind="ExternalOutput").ap()

    ctx = ExitStack()
    with ctx:
        ident = ctx.enter_context(nc.sbuf_tensor("ident_s", [128, 128], BF16))
        onemat = ctx.enter_context(nc.sbuf_tensor("onemat_s", [128, 128], BF16))
        onesrow = ctx.enter_context(nc.sbuf_tensor("onesrow_s", [1, 128], BF16))
        onef = ctx.enter_context(nc.sbuf_tensor("onef_s", [1, 128], F32))
        wqT = ctx.enter_context(nc.sbuf_tensor("wqT_s", [128, 128], BF16))
        wkT = ctx.enter_context(nc.sbuf_tensor("wkT_s", [128, 128], BF16))
        wvT = ctx.enter_context(nc.sbuf_tensor("wvT_s", [128, 128], BF16))
        nqw = ctx.enter_context(nc.sbuf_tensor("nqw_s", [128, EC], F32))
        sgc = ctx.enter_context(nc.sbuf_tensor("sgc_s", [128, IC], F32))
        suc = ctx.enter_context(nc.sbuf_tensor("suc_s", [128, IC], F32))
        beta_s = ctx.enter_context(nc.sbuf_tensor("beta_s", [1, 1], F32))
        bv1 = ctx.enter_context(nc.sbuf_tensor("bv1_s", [1, 1], F32))
        bvec = ctx.enter_context(nc.sbuf_tensor("bvec_s", [128, 1], F32))
        m1s = ctx.enter_context(nc.sbuf_tensor("m1s_s", [128, NT], F32))
        a1s = ctx.enter_context(nc.sbuf_tensor("a1s_s", [128, NT], F32))
        bbs = ctx.enter_context(nc.sbuf_tensor("bbs_s", [128, NT], F32))
        wrk = ctx.enter_context(nc.sbuf_tensor("wrk_s", [128, 224], F32))
        kk = ctx.enter_context(nc.sbuf_tensor("kk_s", [1, 18], F32))
        s2r = ctx.enter_context(nc.sbuf_tensor("s2r_s", [1, M], F32))
        brow = ctx.enter_context(nc.sbuf_tensor("brow_s", [1, M], F32))
        negb = ctx.enter_context(nc.sbuf_tensor("negb_s", [1, M], BF16))
        srow = ctx.enter_context(nc.sbuf_tensor("srow_s", [1, T], F32))
        rscol = ctx.enter_context(nc.sbuf_tensor("rscol_s", [128, 128], F32))

        AR = ctx.enter_context(nc.sbuf_tensor("arena", [128, ARENA_BYTES], U8))
        ARbf = AR.bitcast(BF16)
        ARf = AR.bitcast(F32)

        def bfv(off, n):
            assert off % 2 == 0
            return ARbf[:, off // 2: off // 2 + n]

        def f32v(off, n):
            assert off % 4 == 0
            return ARf[:, off // 4: off // 4 + n]

        regions = []

        def reg(name, off, size, lo, hi):
            for (n2, o2, s2, l2, h2) in regions:
                if not (hi < l2 or h2 < lo):
                    assert off + size <= o2 or o2 + s2 <= off, f"arena overlap {name} vs {n2}"
            assert off + size <= ARENA_BYTES, f"arena overflow {name}"
            regions.append((name, off, size, lo, hi))
            return off

        O_W2T = reg("W2T", 0, 65536, 1, 2)
        O_KT = reg("kT", 0, 32768, 3, 4)
        O_V = reg("v", 32768, 32768, 3, 4)
        O_NQT = reg("nqT", 65536, 32768, 1, 2)
        O_QT = reg("qT", 98304, 32768, 2, 4)
        O_XT = [reg("xt0", 131072, 8192, 1, 1), reg("xt1", 139264, 8192, 1, 1),
                reg("xt2", 98304, 8192, 1, 1)]
        O_NQS = [reg("nqs0", 147456, 4096, 1, 1), reg("nqs1", 151552, 4096, 1, 1)]
        O_XSQD = reg("xsqd", 155648, 8192, 1, 1)
        O_WIN = [reg("win0", 163840, 4096, 1, 1), reg("win1", 167936, 4096, 1, 1)]
        O_WINF = [reg("winf0", 172032, 8192, 1, 1), reg("winf1", 180224, 8192, 1, 1)]
        O_QSQ = [reg("qsq0", 131072, 2048, 2, 2), reg("qsq1", 133120, 2048, 2, 2)]
        O_PAT = [reg("pat0", 65536, 2048, 3, 3), reg("pat1", 67584, 2048, 3, 3)]
        O_PATF = [reg("patf0", 73728, 4096, 3, 3), reg("patf1", 77824, 4096, 3, 3)]
        O_KSQ = [reg("ksq0", 69632, 2048, 3, 3), reg("ksq1", 71680, 2048, 3, 3)]
        O_ET = reg("eT", 73728, 16384, 4, 4)
        O_U = [reg("U0", 90112, 2048, 4, 4), reg("U1", 92160, 2048, 4, 4)]
        O_A = reg("A", 131072, 32768, 4, 5)
        O_XRE = [reg("xre0", 32768, 8192, 5, 5), reg("xre1", 40960, 8192, 5, 5)]
        O_T0 = [reg("t00", 49152, 8192, 5, 5), reg("t01", 57344, 8192, 5, 5)]
        O_SQG = [reg("sqg0", 65536, 8192, 5, 5), reg("sqg1", 73728, 8192, 5, 5)]
        O_T3 = [reg("t3", 81920, 8192, 5, 5), reg("t3b", 122880, 8192, 5, 5)]
        O_X1 = [reg("x1a", 90112, 8192, 5, 5), reg("x1b", 98304, 8192, 5, 5)]
        O_XA = [reg("xa0", 106496, 8192, 5, 5), reg("xa1", 114688, 8192, 5, 5)]
        O_WRREP = reg("wrrep", 172032, 8192, 5, 5)
        O_X1N = [reg("x1n0", 163840, 4096, 5, 5), reg("x1n1", 167936, 4096, 5, 5)]
        O_X1NT = reg("x1nT", 0, 32768, 5, 6)
        O_FUSE = reg("fuse", 32768, 131072, 6, 7)
        O_WG = [reg("wg0", 163840, 4096, 6, 6), reg("wg1", 167936, 4096, 6, 6)]
        O_WU = [reg("wu0", 172032, 4096, 6, 6), reg("wu1", 176128, 4096, 6, 6)]
        O_SG = [reg("sg0", 180224, 2048, 6, 6), reg("sg1", 182272, 2048, 6, 6)]
        O_SU = [reg("su0", 184320, 2048, 6, 6), reg("su1", 186368, 2048, 6, 6)]
        O_WD = [reg("wd%d" % i, i * 4096, 4096, 7, 7) for i in range(6)]
        O_WD8 = [reg("wd8_%d" % i, 24576 + i * 2048, 2048, 7, 7) for i in range(3)]
        O_SDREP = reg("sdrep", 163840, 8192, 7, 7)
        O_OUT = [reg("out0", 172032, 2048, 7, 7), reg("out1", 174080, 2048, 7, 7)]
        O_TMP = [reg("tmp0", 176128, 2048, 7, 7), reg("tmp1", 178176, 2048, 7, 7)]
        O_X1RE = [reg("x1re%d" % i, 180224 + i * 2048, 2048, 7, 7) for i in range(4)]

        PS = ctx.enter_context(nc.psum_tensor("PS", [128, 4096], F32))
        PSbf = PS.bitcast(BF16)

        sems = {}
        for name in ["sA0", "sA1", "sA2", "sB0", "sB1", "sB2", "sB3", "sC0", "sC1",
                     "sW0", "sW1", "sW2", "dX0", "dX1", "dSo0", "dSo1",
                     "sXr0", "sXr1", "sXr2", "sXr3",
                     "dC", "dG", "dS", "cT", "cS", "cV", "cG", "vv", "ss", "bar"]:
            sems[name] = Sem(ctx.enter_context(nc.semaphore(name)))
        sA = [sems["sA0"], sems["sA1"], sems["sA2"]]
        sW = [sems["sW0"], sems["sW1"], sems["sW2"]]
        sB = [sems["sB0"], sems["sB1"], sems["sB2"], sems["sB3"]]
        sCm = [sems["sC0"], sems["sC1"]]
        dC = sems["dC"]
        dG = sems["dG"]
        dS = sems["dS"]
        dX = [sems["dX0"], sems["dX1"]]
        dSo = [sems["dSo0"], sems["dSo1"]]
        sXr = [sems["sXr%d" % i] for i in range(4)]
        cT = sems["cT"]
        cS = sems["cS"]
        cV = sems["cV"]
        cG = sems["cG"]
        vv = sems["vv"]
        ss = sems["ss"]
        bar = sems["bar"]

        ops = {"t": [], "s": [], "v": [], "y": [], "g": []}

        def Rt(f): ops["t"].append(f)

        def Rs(f): ops["s"].append(f)

        def Rv(f): ops["v"].append(f)

        def Ry(f): ops["y"].append(f)

        def Rg(f): ops["g"].append(f)

        def wait(R, sem, val):
            R(lambda En, h=sem.h, v=val: En.wait_ge(h, v))

        nbar = [0]

        def barrier():
            nbar[0] += 1
            v = 3 * nbar[0]
            wait(Rt, cS, cS.n)
            wait(Rt, cV, cV.n)
            Rt(lambda En, h=bar.h: En.matmul(PS[0:1, 4095:4096], onesrow[0:1, 0:1],
                                             onesrow[0:1, 0:1], start=True,
                                             stop=True).then_inc(h, 1))
            Rs(lambda En, h=bar.h: En.activation(wrk[0:1, 30:31], onef[0:1, 0:1],
                                                 AF.Copy).then_inc(h, 1))
            Rv(lambda En, h=bar.h: En.tensor_copy(wrk[0:1, 31:32],
                                                  onef[0:1, 1:2]).then_inc(h, 1))
            for R in (Rt, Rs, Rv):
                wait(R, bar, v)
            bar.n = v

        def VC(*fns, final=None):
            n = len(fns)
            res = None
            for k, fn in enumerate(fns):
                if k > 0:
                    wait(Rv, vv, vv.n)
                sem = (final if (final is not None and k == n - 1) else vv)
                res = sem.tick()
                Rv(lambda En, fn=fn, h=sem.h: fn(En).then_inc(h, 1))
            return res

        def dma(R, sem, dst, src, k=16):
            sem.tick(k)
            R(lambda En, d=dst, s=src, h=sem.h, k=k: En.dma_start(out=d, in_=s).then_inc(h, k))

        dma(Ry, dC, ident[:], id_d)
        dma(Ry, dC, onemat[:], om_d)
        dma(Ry, dC, onesrow[:], or_d)
        dma(Ry, dC, onef[:], of_d)
        dma(Ry, dC, nqw[:], nqw_d)
        dma(Ry, dC, sgc[:], sgc_d)
        dma(Ry, dC, suc[:], suc_d)
        dma(Ry, dC, beta_s[:], beta_d)
        dma(Rg, dG, wqT[:], wqT_d)
        dma(Rg, dG, wkT[:], wkT_d)
        dma(Rg, dG, wvT[:], wvT_d)
        dc_consts = dC.n
        dg_consts = dG.n

        wait(Rs, dC, dc_consts)
        wait(Rs, dG, dg_consts)
        t = cS.tick()
        Rs(lambda En: En.activation(bv1[:], beta_s[:], AF.Sigmoid).then_inc(cS.h, 1))
        wait(Rt, dC, dc_consts)
        wait(Rt, dG, dg_consts)
        wait(Rt, cS, t)
        t = cT.tick()
        Rt(lambda En: En.matmul(PS[:, 0:1], onef[0:1, 0:128], bv1[0:1, 0:1],
                                start=True, stop=True).then_inc(cT.h, 1))
        wait(Rv, cT, t)
        t = cV.tick()
        Rv(lambda En: En.tensor_copy(bvec[:], PS[:, 0:1]).then_inc(cV.h, 1))
        pre_win_ld = []
        pre_b_ld = []
        for i in range(2):
            dma(Ry, sW[i], ARf[:, (172032 + i * 8192) // 4:(172032 + i * 8192) // 4 + 2048],
                win_d[i])
            pre_win_ld.append(sW[i].n)
            dma(Ry, sA[i], ARf[:, (131072 + i * 8192) // 4:(131072 + i * 8192) // 4 + 2048],
                x_d[i])
            pre_b_ld.append(sA[i].n)
        barrier()

        nqT = bfv(O_NQT, 16384)
        nqT3 = nqT.rearrange("p (ec t) -> p ec t", t=T)
        W2T = bfv(O_W2T, 32768)
        W2T3 = W2T.rearrange("p (ec hd) -> p ec hd", hd=E)
        PF = PS[:, 2048:4096]

        nq_evac_val = [None] * NT
        nq_pass_val = [None] * NT
        tr_val = [None] * NT
        chain1_val = [None] * NT
        fold_mm_val = [None] * H
        fold_evac_val = [None] * H

        win_cv = [None] * H
        win_ld = [None] * H

        def win_load(h):
            winf = f32v(O_WINF[h % 2], 2048)
            if h >= 2:
                wait(Ry, cG, win_cv[h - 2])
            dma(Ry, sW[h % 2], winf, win_d[h])
            win_ld[h] = sW[h % 2].n

        def emit_fold(h):
            winb = bfv(O_WIN[h % 2], 2048)
            winf = f32v(O_WINF[h % 2], 2048)
            wait(Rg, sW[h % 2], win_ld[h])
            if h >= 2:
                wait(Rg, cT, fold_mm_val[h - 2])
            tcv = cG.tick()
            Rg(lambda En, winb=winb, winf=winf:
               En.tensor_copy(winb, winf).then_inc(cG.h, 1))
            win_cv[h] = tcv
            if h + 2 < H:
                win_load(h + 2)
            wait(Rt, cG, tcv)
            src3 = PF.rearrange("p (ec d) -> p ec d", d=128)
            dst3 = W2T3[:, :, h * 128:(h + 1) * 128]
            for hf in range(2):
                if h >= 1:
                    wait(Rt, cS, fold_evac_val[h - 1][hf])
                for ec8 in range(8):
                    ec = hf * 8 + ec8
                    t1 = cT.tick()
                    if ec8 == 7:
                        Rt(lambda En, winb=winb, ec=ec:
                           En.matmul(PF[:, ec * 128:(ec + 1) * 128],
                                     winb[:, ec * 128:(ec + 1) * 128],
                                     wqT[:], start=True, stop=True).then_inc(cT.h, 1))
                    else:
                        cT.n -= 1
                        Rt(lambda En, winb=winb, ec=ec:
                           En.matmul(PF[:, ec * 128:(ec + 1) * 128],
                                     winb[:, ec * 128:(ec + 1) * 128],
                                     wqT[:], start=True, stop=True))
                if hf == 1:
                    fold_mm_val[h] = t1
                wait(Rs, cT, t1)
                t2 = cS.tick()
                Rs(lambda En, d=dst3, s=src3, hf=hf:
                   En.activation(d[:, hf * 8:hf * 8 + 8, :],
                                 s[:, hf * 8:hf * 8 + 8, :], AF.Copy).then_inc(cS.h, 1))
                if fold_evac_val[h] is None:
                    fold_evac_val[h] = [None, None]
                fold_evac_val[h][hf] = t2

        def B_load(j):
            xt = f32v(O_XT[j % 3], 2048)
            if j >= 3:
                wait(Ry, cS, nq_pass_val[j - 3])
            dma(Ry, sA[j % 3], xt, x_d[j])
            b_ld_val[j] = sA[j % 3].n

        def B_V1(j):
            xt = f32v(O_XT[j % 3], 2048)
            W = 32 + j * 16
            ST = 176 + (j % 2) * 24
            wait(Rv, sA[j % 3], b_ld_val[j])
            for i in range(4):
                if i == 3:
                    t = vv.tick()
                    Rv(lambda En, i=i, xt=xt, ST=ST:
                       En.bn_stats(wrk[:, ST + i * 6:ST + (i + 1) * 6],
                                   xt[:, i * 512:(i + 1) * 512]).then_inc(vv.h, 1))
                else:
                    Rv(lambda En, i=i, xt=xt, ST=ST:
                       En.bn_stats(wrk[:, ST + i * 6:ST + (i + 1) * 6],
                                   xt[:, i * 512:(i + 1) * 512]))
            wait(Rv, vv, t)
            chain1_val[j] = VC(
                lambda En, W=W, ST=ST: En.bn_aggr(wrk[:, W + 3:W + 5], wrk[:, ST:ST + 24]),
                lambda En, W=W: En.tensor_scalar(wrk[:, W + 5:W + 6], wrk[:, W + 4:W + 5],
                                                 EPS_LN, None, ALU.add),
                lambda En, W=W: En.tensor_scalar(wrk[:, W + 6:W + 7], wrk[:, W + 4:W + 5],
                                                 1.0 + EPS_RMS, EPS_RMS * EPS_LN,
                                                 ALU.mult, ALU.add),
                lambda En, W=W: En.reciprocal(wrk[:, W + 7:W + 8], wrk[:, W + 5:W + 6]),
                lambda En, W=W: En.reciprocal(wrk[:, W + 8:W + 9], wrk[:, W + 6:W + 7]),
                lambda En, W=W: En.tensor_scalar(wrk[:, W + 13:W + 14], wrk[:, W + 3:W + 4],
                                                 -1.0, None, ALU.mult),
                final=cV)

        def B_S2(j):
            W = 32 + j * 16
            wait(Rs, cV, chain1_val[j])
            Rs(lambda En, W=W: En.sqrt(wrk[:, W + 9:W + 10], wrk[:, W + 7:W + 8]))
            t = ss.tick()
            Rs(lambda En, W=W: En.sqrt(wrk[:, W + 10:W + 11],
                                       wrk[:, W + 8:W + 9]).then_inc(ss.h, 1))
            b_sqrt_val[j] = t

        def B_V1b(j):
            W = 32 + j * 16
            wait(Rv, ss, b_sqrt_val[j])
            chain2_val[j] = VC(
                lambda En, W=W: En.tensor_mul(wrk[:, W + 11:W + 12], wrk[:, W + 10:W + 11],
                                              wrk[:, W + 13:W + 14]),
                lambda En, j=j, W=W: En.tensor_mul(bbs[:, j:j + 1], wrk[:, W + 9:W + 10],
                                                   wrk[:, W + 13:W + 14]),
                lambda En, j=j, W=W: En.tensor_scalar(a1s[:, j:j + 1], wrk[:, W + 9:W + 10],
                                                      1.0, None, ALU.add),
                final=cV)

        def B_S2b(j):
            xt = f32v(O_XT[j % 3], 2048)
            nqs = bfv(O_NQS[j % 2], 2048)
            W = 32 + j * 16
            wait(Rs, cV, chain2_val[j])
            if j >= 2:
                wait(Rs, cT, tr_val[j - 2])
            t4 = cS.tick()
            Rs(lambda En, nqs=nqs, xt=xt, W=W:
               En.activation(nqs, xt, AF.Identity, bias=wrk[:, W + 11:W + 12],
                             scale=wrk[:, W + 10:W + 11]).then_inc(cS.h, 1))
            nq_pass_val[j] = t4

        def B_T1(j):
            nqs = bfv(O_NQS[j % 2], 2048)
            wait(Rt, cS, nq_pass_val[j])
            if j >= 2:
                wait(Rt, cS, nq_evac_val[j - 2])
            ptb = PSbf[:, (j % 2) * 2048:(j % 2) * 2048 + 2048]
            for ec in range(EC):
                if ec == EC - 1:
                    t5 = cT.tick()
                    Rt(lambda En, ptb=ptb, nqs=nqs, ec=ec:
                       En.transpose(ptb[:, ec * 128:(ec + 1) * 128],
                                    nqs[:, ec * 128:(ec + 1) * 128], ident[:]).then_inc(cT.h, 1))
                else:
                    Rt(lambda En, ptb=ptb, nqs=nqs, ec=ec:
                       En.transpose(ptb[:, ec * 128:(ec + 1) * 128],
                                    nqs[:, ec * 128:(ec + 1) * 128], ident[:]))
            tr_val[j] = t5

        def B_S3(j):
            ptb = PSbf[:, (j % 2) * 2048:(j % 2) * 2048 + 2048]
            wait(Rs, cT, tr_val[j])
            t6 = cS.tick()
            src3 = ptb.rearrange("p (ec t) -> p ec t", t=128)
            dst3 = nqT3[:, :, j * 128:(j + 1) * 128]
            Rs(lambda En, d=dst3, s=src3: En.activation(d, s, AF.Copy).then_inc(cS.h, 1))
            nq_evac_val[j] = t6

        b_ld_val = [None] * NT
        b_sqrt_val = [None] * NT
        chain2_val = [None] * NT
        win_ld[0], win_ld[1] = pre_win_ld
        b_ld_val[0], b_ld_val[1] = pre_b_ld
        for k in range(NT + 2):
            if 0 <= k - 1 < NT:
                B_S2(k - 1)
                B_V1b(k - 1)
                B_S2b(k - 1)
                B_T1(k - 1)
            if 0 <= k - 2 < NT:
                B_S3(k - 2)
            if k < NT:
                emit_fold(2 * k)
                emit_fold(2 * k + 1)
                if k + 2 < NT:
                    B_load(k + 2)
                B_V1(k)
        barrier()

        qT = bfv(O_QT, 16384)
        q_evac_val = [None] * H
        nqw_val = [None] * EC
        for ec in range(EC):
            t = cS.tick()
            Rs(lambda En, ec=ec: En.activation(nqT3[:, ec, :], nqT3[:, ec, :], AF.Identity,
                                               scale=nqw[:, ec:ec + 1]).then_inc(cS.h, 1))
            nqw_val[ec] = t
        for hc in range(H):
            pq = PS[:, (hc % 4) * 1024:(hc % 4) * 1024 + 1024]
            if hc == 1:
                wait(Rt, cS, nqw_val[EC - 1])
            if hc >= 4:
                wait(Rt, cV, q_evac_val[hc - 4])
            for half in range(2):
                for ec in range(EC):
                    if hc == 0 and half == 0:
                        wait(Rt, cS, nqw_val[ec])
                    lastmm = half == 1 and ec == EC - 1
                    if lastmm:
                        t1 = cT.tick()
                        Rt(lambda En, pq=pq, ec=ec, half=half, hc=hc:
                           En.matmul(pq[:, half * 512:half * 512 + 512],
                                     W2T3[:, ec, hc * 128:(hc + 1) * 128],
                                     nqT3[:, ec, half * 512:half * 512 + 512],
                                     start=(ec == 0), stop=(ec == EC - 1)).then_inc(cT.h, 1))
                    else:
                        Rt(lambda En, pq=pq, ec=ec, half=half, hc=hc:
                           En.matmul(pq[:, half * 512:half * 512 + 512],
                                     W2T3[:, ec, hc * 128:(hc + 1) * 128],
                                     nqT3[:, ec, half * 512:half * 512 + 512],
                                     start=(ec == 0), stop=(ec == EC - 1)))
            wait(Rv, cT, t1)
            t2 = cV.tick()
            Rv(lambda En, hc=hc, pq=pq:
               En.tensor_scalar(qT[:, hc * 1024:(hc + 1) * 1024], pq, bvec[:], None,
                                ALU.mult).then_inc(cV.h, 1))
            q_evac_val[hc] = t2
        barrier()

        kT = bfv(O_KT, 16384)
        vsb = bfv(O_V, 16384)
        wait(Ry, bar, bar.n)
        e_mm_val = [None] * H
        e_kevac_val = [None] * H
        e_evac_val = [None] * H
        pat_cv = [None] * H
        for h in range(H):
            pat = bfv(O_PAT[h % 2], 1024)
            patf = f32v(O_PATF[h % 2], 1024)
            PK = PS[:, (h % 2) * 1024:(h % 2) * 1024 + 1024]
            PV = PS[:, 2048 + (h % 2) * 1024:2048 + (h % 2) * 1024 + 1024]
            if h >= 2:
                wait(Ry, cG, pat_cv[h - 2])
            dma(Ry, sW[h % 2], patf, patT_d[h])
            ldv = sW[h % 2].n
            wait(Rg, sW[h % 2], ldv)
            if h >= 2:
                wait(Rg, cT, e_mm_val[h - 2])
            tcv = cG.tick()
            Rg(lambda En, pat=pat, patf=patf:
               En.tensor_copy(pat, patf).then_inc(cG.h, 1))
            pat_cv[h] = tcv
            wait(Rt, cG, tcv)
            if h >= 2:
                wait(Rt, cS, e_kevac_val[h - 2])
            for half in range(2):
                if half == 1:
                    tk = cT.tick()
                    Rt(lambda En, pat=pat, half=half, PK=PK:
                       En.matmul(PK[:, half * 512:half * 512 + 512], wkT[:],
                                 pat[:, half * 512:half * 512 + 512], start=True,
                                 stop=True).then_inc(cT.h, 1))
                else:
                    Rt(lambda En, pat=pat, half=half, PK=PK:
                       En.matmul(PK[:, half * 512:half * 512 + 512], wkT[:],
                                 pat[:, half * 512:half * 512 + 512], start=True, stop=True))
            if h >= 2:
                wait(Rt, cS, e_evac_val[h - 2])
            for mc in range(MC):
                if mc == MC - 1:
                    t1 = cT.tick()
                    Rt(lambda En, pat=pat, mc=mc, PV=PV:
                       En.matmul(PV[:, mc * 128:(mc + 1) * 128], pat[:, mc * 128:(mc + 1) * 128],
                                 wvT[:], start=True, stop=True).then_inc(cT.h, 1))
                else:
                    Rt(lambda En, pat=pat, mc=mc, PV=PV:
                       En.matmul(PV[:, mc * 128:(mc + 1) * 128], pat[:, mc * 128:(mc + 1) * 128],
                                 wvT[:], start=True, stop=True))
            e_mm_val[h] = t1
            wait(Rs, cT, tk)
            t2 = cS.tick()
            Rs(lambda En, h=h, PK=PK: En.activation(kT[:, h * 1024:(h + 1) * 1024], PK,
                                                    AF.Copy).then_inc(cS.h, 1))
            e_kevac_val[h] = t2
            wait(Rs, cT, t1)
            t3 = cS.tick()
            Rs(lambda En, h=h, PV=PV: En.activation(vsb[:, h * 1024:(h + 1) * 1024], PV,
                                                    AF.Copy).then_inc(cS.h, 1))
            e_evac_val[h] = t3
        barrier()

        Abf = bfv(O_A, 16384)
        A3 = Abf.rearrange("p (j e) -> p j e", e=E)
        eT = bfv(O_ET, 8192)
        PU = PS[:, 2048:3072]
        PSS = PS[:, 3072:4096]
        pTb = PSbf[:, 7 * 1024:8 * 1024]
        exp_vals = []
        uevac_val = [None] * H
        aevac_val = [None] * H
        recip_val = [None] * H
        for h in range(H):
            usb = bfv(O_U[h % 2], 1024)

            def sumU(m, h=h):
                if m == 0 and h >= 1:
                    wait(Rt, cS, aevac_val[h - 1])
                    wait(Rt, cV, recip_val[h - 1])
                wait(Rt, cS, exp_vals[h * MC + m])
                for half in range(2):
                    Rt(lambda En, m=m, half=half:
                       En.matmul(PSS[:, half * 512:half * 512 + 512], onemat[:],
                                 eT[:, m * 1024 + half * 512:m * 1024 + half * 512 + 512],
                                 start=(m == 0), stop=(m == MC - 1)))
                t3 = None
                for half in range(2):
                    lastmm = m == MC - 1 and half == 1
                    if lastmm:
                        t3 = cT.tick()
                        Rt(lambda En, h=h, m=m, half=half:
                           En.matmul(PU[:, half * 512:half * 512 + 512],
                                     vsb[:, (h * MC + m) * 128:(h * MC + m + 1) * 128],
                                     eT[:, m * 1024 + half * 512:m * 1024 + half * 512 + 512],
                                     start=(m == 0), stop=(m == MC - 1)).then_inc(cT.h, 1))
                    else:
                        Rt(lambda En, h=h, m=m, half=half:
                           En.matmul(PU[:, half * 512:half * 512 + 512],
                                     vsb[:, (h * MC + m) * 128:(h * MC + m + 1) * 128],
                                     eT[:, m * 1024 + half * 512:m * 1024 + half * 512 + 512],
                                     start=(m == 0), stop=(m == MC - 1)))
                return t3

            for mc in range(MC):
                K = h * MC + mc
                p = K % 2
                psb = PS[:, p * 1024:p * 1024 + 1024]
                if K >= 2:
                    wait(Rt, cS, exp_vals[K - 2])
                for half in range(2):
                    if half == 1:
                        t1 = cT.tick()
                        Rt(lambda En, psb=psb, h=h, mc=mc, half=half:
                           En.matmul(psb[:, half * 512:half * 512 + 512],
                                     kT[:, h * 1024 + mc * 128:h * 1024 + (mc + 1) * 128],
                                     qT[:, h * 1024 + half * 512:h * 1024 + half * 512 + 512],
                                     start=True, stop=True).then_inc(cT.h, 1))
                    else:
                        Rt(lambda En, psb=psb, h=h, mc=mc, half=half:
                           En.matmul(psb[:, half * 512:half * 512 + 512],
                                     kT[:, h * 1024 + mc * 128:h * 1024 + (mc + 1) * 128],
                                     qT[:, h * 1024 + half * 512:h * 1024 + half * 512 + 512],
                                     start=True, stop=True))
                wait(Rs, cT, t1)
                t2 = cS.tick()
                Rs(lambda En, mc=mc, psb=psb:
                   En.activation(eT[:, mc * 1024:(mc + 1) * 1024], psb, AF.Exp).then_inc(cS.h, 1))
                exp_vals.append(t2)
                if mc >= 1:
                    sumU(mc - 1)
            t3 = sumU(MC - 1)
            wait(Rs, cT, t3)
            tS = cS.tick()
            Rs(lambda En: En.activation(srow[:], PSS[0:1, :], AF.Copy).then_inc(cS.h, 1))
            t5 = cS.tick()
            Rs(lambda En, usb=usb: En.activation(usb, PU, AF.Copy).then_inc(cS.h, 1))
            uevac_val[h] = t5
            wait(Rt, cS, tS)
            for j in range(NT):
                if j == NT - 1:
                    t6 = cT.tick()
                    Rt(lambda En, j=j: En.matmul(PS[:, 3072 + j:3072 + j + 1],
                                                 srow[0:1, j * 128:(j + 1) * 128],
                                                 onef[0:1, 0:1], start=True,
                                                 stop=True).then_inc(cT.h, 1))
                else:
                    Rt(lambda En, j=j: En.matmul(PS[:, 3072 + j:3072 + j + 1],
                                                 srow[0:1, j * 128:(j + 1) * 128],
                                                 onef[0:1, 0:1], start=True, stop=True))
            wait(Rv, cT, t6)
            t7 = cV.tick()
            Rv(lambda En, h=h: En.reciprocal(rscol[:, h * NT:h * NT + NT],
                                             PS[:, 3072:3072 + NT]).then_inc(cV.h, 1))
            recip_val[h] = t7
            wait(Rt, cS, t5)
            for j in range(NT):
                if j == NT - 1:
                    t8 = cT.tick()
                    Rt(lambda En, usb=usb, j=j:
                       En.transpose(pTb[:, j * 128:(j + 1) * 128], usb[:, j * 128:(j + 1) * 128],
                                    ident[:]).then_inc(cT.h, 1))
                else:
                    Rt(lambda En, usb=usb, j=j:
                       En.transpose(pTb[:, j * 128:(j + 1) * 128], usb[:, j * 128:(j + 1) * 128],
                                    ident[:]))
            wait(Rs, cT, t8)
            t9 = cS.tick()
            src3 = pTb.rearrange("p (j d) -> p j d", d=128)
            dst3 = A3[:, :, h * 128:(h + 1) * 128]
            Rs(lambda En, d=dst3, s=src3: En.activation(d, s, AF.Copy).then_inc(cS.h, 1))
            aevac_val[h] = t9
        barrier()

        x1nT = bfv(O_X1NT, 16384)
        x1nT3 = x1nT.rearrange("p (hc t) -> p hc t", t=T)
        wrrep = f32v(O_WRREP, 2048)
        wait(Ry, bar, bar.n)
        dma(Ry, dC, wrrep, wr_d)
        dc_wr = dC.n
        wait(Rg, bar, bar.n)
        g_ld = [None] * NT
        p1val = [None] * NT
        p1sval = [None] * NT
        sq1val = [None] * NT
        v2val = [None] * NT
        rsqv = [None] * NT
        xaval = [None] * NT
        t3val = [None] * NT
        adv_v = [None] * NT
        adv_g = [None] * NT
        dxval = [None] * NT
        bnval = [None] * NT
        x1nval = [None] * NT
        tr2_val = [None] * NT
        g_evac_val = [None] * NT
        sqd = f32v(O_SQG[0], 2048)

        def G_load(j):
            xre = f32v(O_XRE[j % 2], 2048)
            if j >= 2:
                wait(Ry, cS, xaval[j - 2])
            dma(Ry, sA[j % 2], xre, x_d[j])
            g_ld[j] = sA[j % 2].n

        def G_xar(j):
            xre = f32v(O_XRE[j % 2], 2048)
            xar = f32v(O_XA[j % 2], 2048)
            wait(Rs, sA[j % 2], g_ld[j])
            if j >= 2:
                wait(Rs, cG, adv_g[j - 2])
            t = cS.tick()
            Rs(lambda En, xar=xar, xre=xre, j=j:
               En.activation(xar, xre, AF.Identity, scale=a1s[:, j:j + 1],
                             bias=bbs[:, j:j + 1]).then_inc(cS.h, 1))
            xaval[j] = t

        def G_P1(j):
            t0 = f32v(O_T0[j % 2], 2048)
            if j >= 2:
                wait(Rv, cS, sq1val[j - 2])
            for hh in range(7, H):
                if hh == H - 1:
                    t = cV.tick()
                    Rv(lambda En, j=j, hh=hh, t0=t0:
                       En.tensor_scalar(t0[:, hh * 128:(hh + 1) * 128],
                                        A3[:, j, hh * 128:(hh + 1) * 128],
                                        rscol[:, hh * NT + j:hh * NT + j + 1], None,
                                        ALU.mult).then_inc(cV.h, 1))
                else:
                    Rv(lambda En, j=j, hh=hh, t0=t0:
                       En.tensor_scalar(t0[:, hh * 128:(hh + 1) * 128],
                                        A3[:, j, hh * 128:(hh + 1) * 128],
                                        rscol[:, hh * NT + j:hh * NT + j + 1], None,
                                        ALU.mult))
            p1val[j] = t

        def G_P1s(j):
            t0 = f32v(O_T0[j % 2], 2048)
            if j >= 2:
                wait(Rs, cV, t3val[j - 2])
            for hh in range(7):
                if hh == 6:
                    t = ss.tick()
                    Rs(lambda En, j=j, hh=hh, t0=t0:
                       En.activation(t0[:, hh * 128:(hh + 1) * 128],
                                     A3[:, j, hh * 128:(hh + 1) * 128], AF.Identity,
                                     scale=rscol[:, hh * NT + j:hh * NT + j + 1]).then_inc(ss.h, 1))
                    p1sval[j] = t
                else:
                    Rs(lambda En, j=j, hh=hh, t0=t0:
                       En.activation(t0[:, hh * 128:(hh + 1) * 128],
                                     A3[:, j, hh * 128:(hh + 1) * 128], AF.Identity,
                                     scale=rscol[:, hh * NT + j:hh * NT + j + 1]))

        def G_S1(j):
            t0 = f32v(O_T0[j % 2], 2048)
            W = 32 + j * 16
            wait(Rs, cV, p1val[j])
            wait(Rs, ss, p1sval[j])
            t = cS.tick()
            Rs(lambda En, t0=t0, W=W:
               En.activation(sqd, t0, AF.Square,
                             accum_out=wrk[:, W:W + 1]).then_inc(cS.h, 1))
            sq1val[j] = t

        def G_V2(j):
            W = 32 + j * 16
            wait(Rv, cS, sq1val[j])
            v2val[j] = VC(
                lambda En, W=W: En.tensor_scalar(wrk[:, W + 1:W + 2], wrk[:, W:W + 1],
                                                 1.0 / E, EPS_RMS, ALU.mult, ALU.add),
                lambda En, W=W: En.reciprocal(wrk[:, W + 2:W + 3], wrk[:, W + 1:W + 2]),
                final=cV)

        def G_sqrt(j):
            W = 32 + j * 16
            wait(Rs, cV, v2val[j])
            t = ss.tick()
            Rs(lambda En, W=W: En.sqrt(wrk[:, W + 3:W + 4], wrk[:, W + 2:W + 3]).then_inc(ss.h, 1))
            rsqv[j] = t

        def G_T3(j):
            t0 = f32v(O_T0[j % 2], 2048)
            t3r = f32v(O_T3[j % 2], 2048)
            W = 32 + j * 16
            wait(Rv, ss, rsqv[j])
            wait(Rv, dC, dc_wr)
            if j >= 2:
                wait(Rv, cG, adv_g[j - 2])
            t = cV.tick()
            Rv(lambda En, t3r=t3r, t0=t0, W=W:
               En.scalar_tensor_tensor(t3r, t0, wrk[:, W + 3:W + 4], wrrep,
                                       ALU.mult, ALU.mult).then_inc(cV.h, 1))
            t3val[j] = t

        def G_AD(j):
            t3r = f32v(O_T3[j % 2], 2048)
            xar = f32v(O_XA[j % 2], 2048)
            x1t = f32v(O_X1[j % 2], 2048)
            wait(Rg, cV, t3val[j])
            wait(Rg, cS, xaval[j])
            if j >= 2:
                wait(Rg, cV, bnval[j - 2])
                wait(Rg, cS, x1nval[j - 2])
                wait(Rg, dX[j % 2], dxval[j - 2])
            t = cG.tick()
            Rg(lambda En, x1t=x1t, t3r=t3r, xar=xar:
               En.tensor_add(x1t, t3r, xar).then_inc(cG.h, 1))
            adv_g[j] = t
            adv_v[j] = t

        def G_SP(j):
            x1t = f32v(O_X1[j % 2], 2048)
            wait(Ry, cG, adv_g[j])
            dma(Ry, dX[j % 2], x1sp_d[j], x1t)
            dxval[j] = dX[j % 2].n

        def G_BN(j):
            x1t = f32v(O_X1[j % 2], 2048)
            W = 32 + j * 16
            ST = 176 + (j % 2) * 24
            wait(Rv, cG, adv_g[j])
            for i in range(4):
                if i == 3:
                    t = vv.tick()
                    Rv(lambda En, i=i, x1t=x1t, ST=ST:
                       En.bn_stats(wrk[:, ST + i * 6:ST + (i + 1) * 6],
                                   x1t[:, i * 512:(i + 1) * 512]).then_inc(vv.h, 1))
                else:
                    Rv(lambda En, i=i, x1t=x1t, ST=ST:
                       En.bn_stats(wrk[:, ST + i * 6:ST + (i + 1) * 6],
                                   x1t[:, i * 512:(i + 1) * 512]))
            wait(Rv, vv, t)
            bnval[j] = VC(
                lambda En, W=W, ST=ST: En.bn_aggr(wrk[:, W + 6:W + 8], wrk[:, ST:ST + 24]),
                lambda En, W=W: En.tensor_scalar(wrk[:, W + 8:W + 9], wrk[:, W + 7:W + 8],
                                                 EPS_LN, None, ALU.add),
                lambda En, W=W: En.reciprocal(wrk[:, W + 9:W + 10], wrk[:, W + 8:W + 9]),
                final=cV)

        def G_SQ(j):
            x1t = f32v(O_X1[j % 2], 2048)
            x1n = bfv(O_X1N[j % 2], 2048)
            W = 32 + j * 16
            wait(Rs, cV, bnval[j])
            t = ss.tick()
            Rs(lambda En, W=W: En.sqrt(wrk[:, W + 10:W + 11],
                                       wrk[:, W + 9:W + 10]).then_inc(ss.h, 1))
            wait(Rs, ss, t)
            t = ss.tick()
            Rs(lambda En, W=W: En.mul(wrk[:, W + 12:W + 13], wrk[:, W + 10:W + 11],
                                      -1.0).then_inc(ss.h, 1))
            wait(Rs, ss, t)
            t = ss.tick()
            Rs(lambda En, W=W: En.mul(wrk[:, W + 11:W + 12], wrk[:, W + 12:W + 13],
                                      wrk[:, W + 6:W + 7]).then_inc(ss.h, 1))
            wait(Rs, ss, t)
            if j >= 2:
                wait(Rs, cT, tr2_val[j - 2])
            wait(Rs, cG, adv_g[j])
            t = cS.tick()
            Rs(lambda En, x1n=x1n, x1t=x1t, W=W:
               En.activation(x1n, x1t, AF.Identity, scale=wrk[:, W + 10:W + 11],
                             bias=wrk[:, W + 11:W + 12]).then_inc(cS.h, 1))
            x1nval[j] = t

        def G_T1(j):
            x1n = bfv(O_X1N[j % 2], 2048)
            wait(Rt, cS, x1nval[j])
            if j >= 2:
                wait(Rt, cV, g_evac_val[j - 2])
            ptb = PSbf[:, (j % 2) * 2048:(j % 2) * 2048 + 2048]
            for ec in range(EC):
                if ec == EC - 1:
                    tH = cT.tick()
                    Rt(lambda En, ptb=ptb, x1n=x1n, ec=ec:
                       En.transpose(ptb[:, ec * 128:(ec + 1) * 128],
                                    x1n[:, ec * 128:(ec + 1) * 128], ident[:]).then_inc(cT.h, 1))
                else:
                    Rt(lambda En, ptb=ptb, x1n=x1n, ec=ec:
                       En.transpose(ptb[:, ec * 128:(ec + 1) * 128],
                                    x1n[:, ec * 128:(ec + 1) * 128], ident[:]))
            tr2_val[j] = tH

        def G_SE(j):
            ptb = PSbf[:, (j % 2) * 2048:(j % 2) * 2048 + 2048]
            wait(Rv, cT, tr2_val[j])
            tI = cV.tick()
            src3 = ptb.rearrange("p (ec t) -> p ec t", t=128)
            dst3 = x1nT3[:, :, j * 128:(j + 1) * 128]
            Rv(lambda En, d=dst3, s=src3: En.tensor_copy(d, s).then_inc(cV.h, 1))
            g_evac_val[j] = tI

        G_load(0)
        G_load(1)
        for k in range(NT + 2):
            if 0 <= k - 1 < NT:
                G_V2(k - 1)
                G_sqrt(k - 1)
            if 0 <= k - 2 < NT:
                G_BN(k - 2)
                G_SQ(k - 2)
                G_T1(k - 2)
            if k < NT:
                G_xar(k)
                G_P1(k)
                G_P1s(k)
                G_S1(k)
                if k + 2 < NT:
                    G_load(k + 2)
            if 0 <= k - 1 < NT:
                G_T3(k - 1)
                G_AD(k - 1)
                G_SP(k - 1)
            if 0 <= k - 2 < NT:
                G_SE(k - 2)
        barrier()

        fuse = bfv(O_FUSE, 65536)
        fuse3 = fuse.rearrange("p (ic t) -> p ic t", t=T)
        wait(Rv, dX[0], dX[0].n)
        wait(Rv, dX[1], dX[1].n)
        h_mm_val = [None] * IC
        h_sg_val = [None] * IC
        h_fuse_val = [None] * IC
        wait(Rg, bar, bar.n)
        for ic in range(IC):
            wgb = bfv(O_WG[ic % 2], 2048)
            wub = bfv(O_WU[ic % 2], 2048)
            sgb = bfv(O_SG[ic % 2], 1024)
            sub = bfv(O_SU[ic % 2], 1024)
            if ic >= 2:
                wait(Rg, cT, h_mm_val[ic - 2])
            dma(Rg, sB[ic % 2], wgb, wg_d[ic])
            ldg = sB[ic % 2].n
            dma(Rg, sCm[ic % 2], wub, wu_d[ic])
            ldu = sCm[ic % 2].n
            p = ic % 2
            PG = PS[:, p * 1024:p * 1024 + 1024]
            PU2 = PS[:, 2048 + p * 1024:2048 + p * 1024 + 1024]
            wait(Rt, sB[ic % 2], ldg)
            wait(Rt, sCm[ic % 2], ldu)
            if ic >= 2:
                wait(Rt, cS, h_sg_val[ic - 2])
            for half in range(2):
                for hc in range(EC):
                    Rt(lambda En, PG=PG, wgb=wgb, hc=hc, half=half:
                       En.matmul(PG[:, half * 512:half * 512 + 512], wgb[:, hc * 128:(hc + 1) * 128],
                                 x1nT3[:, hc, half * 512:half * 512 + 512],
                                 start=(hc == 0), stop=(hc == EC - 1)))
            for half in range(2):
                for hc in range(EC):
                    lastmm = half == 1 and hc == EC - 1
                    if lastmm:
                        t1 = cT.tick()
                        Rt(lambda En, PU2=PU2, wub=wub, hc=hc, half=half:
                           En.matmul(PU2[:, half * 512:half * 512 + 512],
                                     wub[:, hc * 128:(hc + 1) * 128],
                                     x1nT3[:, hc, half * 512:half * 512 + 512],
                                     start=(hc == 0), stop=(hc == EC - 1)).then_inc(cT.h, 1))
                    else:
                        Rt(lambda En, PU2=PU2, wub=wub, hc=hc, half=half:
                           En.matmul(PU2[:, half * 512:half * 512 + 512],
                                     wub[:, hc * 128:(hc + 1) * 128],
                                     x1nT3[:, hc, half * 512:half * 512 + 512],
                                     start=(hc == 0), stop=(hc == EC - 1)))
            h_mm_val[ic] = t1
            wait(Rs, cT, t1)
            if ic >= 2:
                wait(Rs, cV, h_fuse_val[ic - 2])
            Rs(lambda En, sgb=sgb, PG=PG, ic=ic:
               En.activation(sgb, PG, AF.Silu, scale=sgc[:, ic:ic + 1]))
            t2 = cS.tick()
            Rs(lambda En, sub=sub, PU2=PU2, ic=ic:
               En.activation(sub, PU2, AF.Identity, scale=suc[:, ic:ic + 1]).then_inc(cS.h, 1))
            h_sg_val[ic] = t2
            wait(Rv, cS, t2)
            t3 = cV.tick()
            Rv(lambda En, ic=ic, sgb=sgb, sub=sub:
               En.tensor_mul(fuse3[:, ic, :], sgb, sub).then_inc(cV.h, 1))
            h_fuse_val[ic] = t3
        barrier()

        sdrep = f32v(O_SDREP, 2048)
        wait(Ry, bar, bar.n)
        dma(Ry, dC, sdrep, sd_d)
        dc_sd = dC.n
        store_i = 0
        out_done = []
        i_mm_val = {}
        ep_j = [[None] * NT for _ in range(HQ)]
        wait(Rg, bar, bar.n)
        wait(Ry, dX[0], dX[0].n)
        wait(Ry, dX[1], dX[1].n)
        wd_cv = {}
        ep_list = []
        bank_done = [[None] * NT for _ in range(HQ)]
        store_ic = [0]

        def I_x1re_load(q, j):
            x1re = f32v(O_X1RE[j % 4], 512)
            idx = q * NT + j
            if idx >= 4:
                wait(Rg, cV, ep_list[idx - 4])
            dma(Rg, sXr[j % 4], x1re, x1sp_d[j][:, q * 512:(q + 1) * 512])
            x1re_ld[idx] = sXr[j % 4].n

        def I_epilogue(q, j):
            store_i = store_ic[0]
            x1re = f32v(O_X1RE[j % 4], 512)
            tmp = f32v(O_TMP[store_i % 2], 512)
            outb = f32v(O_OUT[store_i % 2], 512)
            idx = q * NT + j
            wait(Rv, cT, bank_done[q][j])
            wait(Rv, sXr[j % 4], x1re_ld[idx])
            wait(Rv, dC, dc_sd)
            if store_i >= 2:
                wait(Rv, cV, out_done[store_i - 2])
                wait(Rv, dSo[store_i % 2], dSo[store_i % 2].n)
            t = vv.tick()
            Rv(lambda En, j=j, q=q, tmp=tmp:
               En.tensor_mul(tmp, PS[:, j * 512:(j + 1) * 512],
                             sdrep[:, q * 512:(q + 1) * 512]).then_inc(vv.h, 1))
            wait(Rv, vv, t)
            t2 = cV.tick()
            Rv(lambda En, outb=outb, tmp=tmp, x1re=x1re:
               En.tensor_add(outb, tmp, x1re).then_inc(cV.h, 1))
            out_done.append(t2)
            ep_list.append(t2)
            ep_j[q][j] = t2
            wait(Rg, cV, t2)
            dma(Rg, dSo[store_i % 2], out_d[j][:, q * 512:(q + 1) * 512], outb)
            store_ic[0] += 1

        def I_load(gi):
            q, g = divmod(gi, 16)
            i8s = gi % 3
            wd8 = AR.bitcast(I8)[:, O_WD8[i8s]:O_WD8[i8s] + 2048]
            if gi >= 3:
                wait(Ry, cS, wd_cv[gi - 3])
            dma(Ry, sW[i8s], wd8, wd_d[q, g])
            wd_ld[gi] = sW[i8s].n

        def I_convert(gi):
            i8s = gi % 3
            bfs = gi % 6
            wd8 = AR.bitcast(I8)[:, O_WD8[i8s]:O_WD8[i8s] + 2048]
            wdb = bfv(O_WD[bfs], 2048)
            wait(Rs, sW[i8s], wd_ld[gi])
            if gi >= 6:
                pq_, pg_ = divmod(gi - 6, 16)
                wait(Rs, cT, i_mm_val[(pq_, pg_)])
            tcv = cS.tick()
            Rs(lambda En, wdb=wdb, wd8=wd8:
               En.activation(wdb, wd8, AF.Copy).then_inc(cS.h, 1))
            wd_cv[gi] = tcv

        def I_group(gi):
            q, g = divmod(gi, 16)
            bfs = gi % 6
            wdb = bfv(O_WD[bfs], 2048)
            wait(Rt, cS, wd_cv[gi])
            if g == 15:
                for j in range(4):
                    I_x1re_load(q, j)
                order = [(c, j) for j in range(NT) for c in range(4)]
            else:
                order = [(c, j) for c in range(4) for j in range(NT)]
            for (c, j) in order:
                ic = g * 4 + c
                if q > 0 and ic == 0:
                    wait(Rt, cV, ep_j[q - 1][j])
                tick = (ic == IC - 1) or (g < 15 and c == 3 and j == NT - 1)
                if tick:
                    t1 = cT.tick()
                    Rt(lambda En, ic=ic, j=j, c=c, wdb=wdb:
                       En.matmul(PS[:, j * 512:(j + 1) * 512],
                                 fuse3[:, ic, j * 128:(j + 1) * 128],
                                 wdb[:, c * 512:(c + 1) * 512],
                                 start=(ic == 0), stop=(ic == IC - 1)).then_inc(cT.h, 1))
                    if ic == IC - 1:
                        bank_done[q][j] = t1
                        I_epilogue(q, j)
                        if j + 4 < NT:
                            I_x1re_load(q, j + 4)
                else:
                    Rt(lambda En, ic=ic, j=j, c=c, wdb=wdb:
                       En.matmul(PS[:, j * 512:(j + 1) * 512],
                                 fuse3[:, ic, j * 128:(j + 1) * 128],
                                 wdb[:, c * 512:(c + 1) * 512],
                                 start=(ic == 0), stop=(ic == IC - 1)))
            i_mm_val[(q, g)] = cT.n

        wd_ld = {}
        x1re_ld = {}
        for gi in range(HQ * 16):
            I_load(gi)
            I_convert(gi)
            if gi >= 2:
                I_group(gi - 2)
        I_group(HQ * 16 - 2)
        I_group(HQ * 16 - 1)
        wait(Ry, dX[0], dX[0].n)
        wait(Ry, dX[1], dX[1].n)
        wait(Ry, dSo[0], dSo[0].n)
        wait(Ry, dSo[1], dSo[1].n)

        blk = ctx.enter_context(nc.Block())
        blk.tensor(lambda En: [f(En) for f in ops["t"]])
        blk.scalar(lambda En: [f(En) for f in ops["s"]])
        blk.vector(lambda En: [f(En) for f in ops["v"]])
        blk.sync(lambda En: [f(En) for f in ops["y"]])
        blk.gpsimd(lambda En: [f(En) for f in ops["g"]])
    return nc


_NC_CACHE = [None]


def _get_nc():
    if _NC_CACHE[0] is None:
        _NC_CACHE[0] = _build_nc()
    return _NC_CACHE[0]


def _prepare(L0_attn_query_input, L0_attn_W_in, L0_attn_W_q, L0_attn_W_k, L0_attn_W_v,
             L0_attn_stored_patterns, L0_attn_norm_query_w, L0_attn_norm_retrieved_w,
             L0_attn_beta, L0_ffn_w_gate, L0_ffn_w_up, L0_ffn_w_down, L0_ffn_s_gate,
             L0_ffn_s_up, L0_ffn_s_down):
    x = np.ascontiguousarray(np.asarray(L0_attn_query_input, dtype=np.float32))
    B, S, E_ = x.shape
    xt = x.reshape(B * S, E_)

    w_in = np.ascontiguousarray(np.asarray(L0_attn_W_in, dtype=np.float32)).reshape(H, 128, E)
    wqT = np.ascontiguousarray(np.asarray(L0_attn_W_q, dtype=np.float32).T)
    wkT = np.ascontiguousarray(np.asarray(L0_attn_W_k, dtype=np.float32).T)
    wvT = np.ascontiguousarray(np.asarray(L0_attn_W_v, dtype=np.float32).T)
    patT = np.ascontiguousarray(np.asarray(L0_attn_stored_patterns, dtype=np.float32).transpose(0, 2, 1))
    nqw = np.ascontiguousarray(np.asarray(L0_attn_norm_query_w, dtype=np.float32).reshape(EC, 128).T)
    wr = np.ascontiguousarray(np.broadcast_to(np.asarray(L0_attn_norm_retrieved_w, dtype=np.float32),
                                              (128, E)))
    sd = np.ascontiguousarray(np.broadcast_to(np.asarray(L0_ffn_s_down, dtype=np.float32), (128, E)))
    sgc = np.ascontiguousarray(np.asarray(L0_ffn_s_gate, dtype=np.float32).reshape(IC, 128).T)
    suc = np.ascontiguousarray(np.asarray(L0_ffn_s_up, dtype=np.float32).reshape(IC, 128).T)
    beta = np.asarray(L0_attn_beta, dtype=np.float32).reshape(1, 1)
    wg8 = np.asarray(L0_ffn_w_gate, dtype=np.int8)
    wu8 = np.asarray(L0_ffn_w_up, dtype=np.int8)
    wd8 = np.asarray(L0_ffn_w_down, dtype=np.int8)
    wg_blk = np.ascontiguousarray(wg8.reshape(EC, 128, IC, 128).transpose(2, 1, 0, 3)).reshape(IC, 128, E)
    wu_blk = np.ascontiguousarray(wu8.reshape(EC, 128, IC, 128).transpose(2, 1, 0, 3)).reshape(IC, 128, E)
    wd_blk = np.ascontiguousarray(wd8.reshape(16, 4, 128, HQ, 512).transpose(3, 0, 2, 1, 4)).reshape(
        HQ, 16, 128, E)

    ident = np.eye(128, dtype=ml_dtypes.bfloat16)
    onemat = np.ones((128, 128), dtype=ml_dtypes.bfloat16)
    onesrow = np.ones((1, 128), dtype=ml_dtypes.bfloat16)
    onef = np.ones((1, 128), dtype=np.float32)

    shared = dict(win=w_in, wqT=wqT, wkT=wkT, wvT=wvT, patT=patT, nqw=nqw, wr=wr, sd=sd,
                  sgc=sgc, suc=suc, beta=beta, wg=wg_blk, wu=wu_blk, wd=wd_blk,
                  ident=ident, onemat=onemat, onesrow=onesrow, onef=onef)
    in_maps = []
    for c in range(NCORE):
        xs = np.ascontiguousarray(xt[c * T:(c + 1) * T].reshape(NT, 128, E))
        in_maps.append(dict(shared, x=xs))
    return in_maps, (B, S, E_)


def kernel(**inputs):
    in_maps, (B, S, E_) = _prepare(**inputs)
    nc = _get_nc()
    res = run_bass_kernel_spmd(nc, in_maps, core_ids=list(range(NCORE)))
    outs = [np.asarray(res.results[c]["out"], dtype=np.float32).reshape(T, E_) for c in range(NCORE)]
    return np.concatenate(outs, axis=0).reshape(B, S, E_)


# version 20 (final)
# speedup vs baseline: 1.0136x; 1.0136x over previous
import numpy as np
import ml_dtypes
from contextlib import ExitStack

import concourse.bass as bass
import concourse.mybir as mybir
from concourse.bass_utils import run_bass_kernel_spmd

F32 = mybir.dt.float32
BF16 = mybir.dt.bfloat16
I8 = mybir.dt.int8
U8 = mybir.dt.uint8
AF = mybir.ActivationFunctionType
ALU = mybir.AluOpType
AX = mybir.AxisListType

NCORE = 8
T = 1024
NT = 8
E = 2048
EC = 16
H = 16
D = 128
M = 1024
MC = 8
II = 8192
IC = 64
HQ = 4

EPS_LN = 1e-5
EPS_RMS = 1e-6

ARENA_BYTES = 188416


class Sem:
    def __init__(self, h):
        self.h = h
        self.n = 0

    def tick(self, k=1):
        self.n += k
        return self.n


def _build_nc():
    nc = bass.Bass("TRN2", target_bir_lowering=False)

    x_d = nc.dram_tensor("x", [NT, 128, E], F32, kind="ExternalInput").ap()
    win_d = nc.dram_tensor("win", [H, 128, E], F32, kind="ExternalInput").ap()
    wqT_d = nc.dram_tensor("wqT", [128, 128], F32, kind="ExternalInput").ap()
    wkT_d = nc.dram_tensor("wkT", [128, 128], F32, kind="ExternalInput").ap()
    wvT_d = nc.dram_tensor("wvT", [128, 128], F32, kind="ExternalInput").ap()
    patT_d = nc.dram_tensor("patT", [H, 128, M], F32, kind="ExternalInput").ap()
    nqw_d = nc.dram_tensor("nqw", [128, EC], F32, kind="ExternalInput").ap()
    wr_d = nc.dram_tensor("wr", [128, E], F32, kind="ExternalInput").ap()
    sd_d = nc.dram_tensor("sd", [128, E], F32, kind="ExternalInput").ap()
    sgc_d = nc.dram_tensor("sgc", [128, IC], F32, kind="ExternalInput").ap()
    suc_d = nc.dram_tensor("suc", [128, IC], F32, kind="ExternalInput").ap()
    beta_d = nc.dram_tensor("beta", [1, 1], F32, kind="ExternalInput").ap()
    wg_d = nc.dram_tensor("wg", [IC, 128, E], I8, kind="ExternalInput").ap()
    wu_d = nc.dram_tensor("wu", [IC, 128, E], I8, kind="ExternalInput").ap()
    wd_d = nc.dram_tensor("wd", [HQ, 16, 128, E], I8, kind="ExternalInput").ap()
    id_d = nc.dram_tensor("ident", [128, 128], BF16, kind="ExternalInput").ap()
    om_d = nc.dram_tensor("onemat", [128, 128], BF16, kind="ExternalInput").ap()
    or_d = nc.dram_tensor("onesrow", [1, 128], BF16, kind="ExternalInput").ap()
    of_d = nc.dram_tensor("onef", [1, 128], F32, kind="ExternalInput").ap()

    out_d = nc.dram_tensor("out", [NT, 128, E], F32, kind="ExternalOutput").ap()
    x1sp_d = nc.dram_tensor("x1sp", [NT, 128, E], F32, kind="ExternalOutput").ap()

    ctx = ExitStack()
    with ctx:
        ident = ctx.enter_context(nc.sbuf_tensor("ident_s", [128, 128], BF16))
        onemat = ctx.enter_context(nc.sbuf_tensor("onemat_s", [128, 128], BF16))
        onesrow = ctx.enter_context(nc.sbuf_tensor("onesrow_s", [1, 128], BF16))
        onef = ctx.enter_context(nc.sbuf_tensor("onef_s", [1, 128], F32))
        wqT = ctx.enter_context(nc.sbuf_tensor("wqT_s", [128, 128], BF16))
        wkT = ctx.enter_context(nc.sbuf_tensor("wkT_s", [128, 128], BF16))
        wvT = ctx.enter_context(nc.sbuf_tensor("wvT_s", [128, 128], BF16))
        nqw = ctx.enter_context(nc.sbuf_tensor("nqw_s", [128, EC], F32))
        sgc = ctx.enter_context(nc.sbuf_tensor("sgc_s", [128, IC], F32))
        suc = ctx.enter_context(nc.sbuf_tensor("suc_s", [128, IC], F32))
        beta_s = ctx.enter_context(nc.sbuf_tensor("beta_s", [1, 1], F32))
        bv1 = ctx.enter_context(nc.sbuf_tensor("bv1_s", [1, 1], F32))
        bvec = ctx.enter_context(nc.sbuf_tensor("bvec_s", [128, 1], F32))
        m1s = ctx.enter_context(nc.sbuf_tensor("m1s_s", [128, NT], F32))
        a1s = ctx.enter_context(nc.sbuf_tensor("a1s_s", [128, NT], F32))
        bbs = ctx.enter_context(nc.sbuf_tensor("bbs_s", [128, NT], F32))
        wrk = ctx.enter_context(nc.sbuf_tensor("wrk_s", [128, 224], F32))
        kk = ctx.enter_context(nc.sbuf_tensor("kk_s", [1, 18], F32))
        s2r = ctx.enter_context(nc.sbuf_tensor("s2r_s", [1, M], F32))
        brow = ctx.enter_context(nc.sbuf_tensor("brow_s", [1, M], F32))
        negb = ctx.enter_context(nc.sbuf_tensor("negb_s", [1, M], BF16))
        srow = ctx.enter_context(nc.sbuf_tensor("srow_s", [1, T], F32))
        rscol = ctx.enter_context(nc.sbuf_tensor("rscol_s", [128, 128], F32))

        AR = ctx.enter_context(nc.sbuf_tensor("arena", [128, ARENA_BYTES], U8))
        ARbf = AR.bitcast(BF16)
        ARf = AR.bitcast(F32)

        def bfv(off, n):
            assert off % 2 == 0
            return ARbf[:, off // 2: off // 2 + n]

        def f32v(off, n):
            assert off % 4 == 0
            return ARf[:, off // 4: off // 4 + n]

        regions = []

        def reg(name, off, size, lo, hi):
            for (n2, o2, s2, l2, h2) in regions:
                if not (hi < l2 or h2 < lo):
                    assert off + size <= o2 or o2 + s2 <= off, f"arena overlap {name} vs {n2}"
            assert off + size <= ARENA_BYTES, f"arena overflow {name}"
            regions.append((name, off, size, lo, hi))
            return off

        O_W2T = reg("W2T", 0, 65536, 1, 2)
        O_KT = reg("kT", 0, 32768, 3, 4)
        O_V = reg("v", 32768, 32768, 3, 4)
        O_NQT = reg("nqT", 65536, 32768, 1, 2)
        O_QT = reg("qT", 98304, 32768, 2, 4)
        O_XT = [reg("xt0", 131072, 8192, 1, 1), reg("xt1", 139264, 8192, 1, 1),
                reg("xt2", 98304, 8192, 1, 1)]
        O_NQS = [reg("nqs0", 147456, 4096, 1, 1), reg("nqs1", 151552, 4096, 1, 1)]
        O_XSQD = reg("xsqd", 155648, 8192, 1, 1)
        O_WIN = [reg("win0", 163840, 4096, 1, 1), reg("win1", 167936, 4096, 1, 1)]
        O_WINF = [reg("winf0", 172032, 8192, 1, 1), reg("winf1", 180224, 8192, 1, 1)]
        O_QSQ = [reg("qsq0", 131072, 2048, 2, 2), reg("qsq1", 133120, 2048, 2, 2)]
        O_PAT = [reg("pat0", 65536, 2048, 3, 3), reg("pat1", 67584, 2048, 3, 3)]
        O_PATF = [reg("patf0", 73728, 4096, 3, 3), reg("patf1", 77824, 4096, 3, 3)]
        O_KSQ = [reg("ksq0", 69632, 2048, 3, 3), reg("ksq1", 71680, 2048, 3, 3)]
        O_ET = reg("eT", 73728, 16384, 4, 4)
        O_U = [reg("U0", 90112, 2048, 4, 4), reg("U1", 92160, 2048, 4, 4)]
        O_A = reg("A", 131072, 32768, 4, 5)
        O_XRE = [reg("xre0", 32768, 8192, 5, 5), reg("xre1", 40960, 8192, 5, 5)]
        O_T0 = [reg("t00", 49152, 8192, 5, 5), reg("t01", 57344, 8192, 5, 5)]
        O_SQG = [reg("sqg0", 65536, 8192, 5, 5), reg("sqg1", 73728, 8192, 5, 5)]
        O_T3 = [reg("t3", 81920, 8192, 5, 5), reg("t3b", 122880, 8192, 5, 5)]
        O_X1 = [reg("x1a", 90112, 8192, 5, 5), reg("x1b", 98304, 8192, 5, 5)]
        O_XA = [reg("xa0", 106496, 8192, 5, 5), reg("xa1", 114688, 8192, 5, 5)]
        O_WRREP = reg("wrrep", 172032, 8192, 5, 5)
        O_X1N = [reg("x1n0", 163840, 4096, 5, 5), reg("x1n1", 167936, 4096, 5, 5)]
        O_X1NT = reg("x1nT", 0, 32768, 5, 6)
        O_FUSE = reg("fuse", 32768, 131072, 6, 7)
        O_WG = [reg("wg0", 163840, 4096, 6, 6), reg("wg1", 167936, 4096, 6, 6)]
        O_WU = [reg("wu0", 172032, 4096, 6, 6), reg("wu1", 176128, 4096, 6, 6)]
        O_SG = [reg("sg0", 180224, 2048, 6, 6), reg("sg1", 182272, 2048, 6, 6)]
        O_SU = [reg("su0", 184320, 2048, 6, 6), reg("su1", 186368, 2048, 6, 6)]
        O_WD = [reg("wd%d" % i, i * 4096, 4096, 7, 7) for i in range(6)]
        O_WD8 = [reg("wd8_%d" % i, 24576 + i * 2048, 2048, 7, 7) for i in range(3)]
        O_SDREP = reg("sdrep", 163840, 8192, 7, 7)
        O_OUT = [reg("out0", 172032, 2048, 7, 7), reg("out1", 174080, 2048, 7, 7)]
        O_TMP = [reg("tmp0", 176128, 2048, 7, 7), reg("tmp1", 178176, 2048, 7, 7)]
        O_X1RE = [reg("x1re%d" % i, 180224 + i * 2048, 2048, 7, 7) for i in range(4)]

        PS = ctx.enter_context(nc.psum_tensor("PS", [128, 4096], F32))
        PSbf = PS.bitcast(BF16)

        sems = {}
        for name in ["sA0", "sA1", "sA2", "sB0", "sB1", "sB2", "sB3", "sC0", "sC1",
                     "sW0", "sW1", "sW2", "dX0", "dX1", "dSo0", "dSo1",
                     "sXr0", "sXr1", "sXr2", "sXr3",
                     "dC", "dG", "dS", "cT", "cS", "cV", "cG", "vv", "ss", "bar"]:
            sems[name] = Sem(ctx.enter_context(nc.semaphore(name)))
        sA = [sems["sA0"], sems["sA1"], sems["sA2"]]
        sW = [sems["sW0"], sems["sW1"], sems["sW2"]]
        sB = [sems["sB0"], sems["sB1"], sems["sB2"], sems["sB3"]]
        sCm = [sems["sC0"], sems["sC1"]]
        dC = sems["dC"]
        dG = sems["dG"]
        dS = sems["dS"]
        dX = [sems["dX0"], sems["dX1"]]
        dSo = [sems["dSo0"], sems["dSo1"]]
        sXr = [sems["sXr%d" % i] for i in range(4)]
        cT = sems["cT"]
        cS = sems["cS"]
        cV = sems["cV"]
        cG = sems["cG"]
        vv = sems["vv"]
        ss = sems["ss"]
        bar = sems["bar"]

        ops = {"t": [], "s": [], "v": [], "y": [], "g": []}

        def Rt(f): ops["t"].append(f)

        def Rs(f): ops["s"].append(f)

        def Rv(f): ops["v"].append(f)

        def Ry(f): ops["y"].append(f)

        def Rg(f): ops["g"].append(f)

        def wait(R, sem, val):
            R(lambda En, h=sem.h, v=val: En.wait_ge(h, v))

        nbar = [0]

        def barrier():
            nbar[0] += 1
            v = 3 * nbar[0]
            wait(Rt, cS, cS.n)
            wait(Rt, cV, cV.n)
            Rt(lambda En, h=bar.h: En.matmul(PS[0:1, 4095:4096], onesrow[0:1, 0:1],
                                             onesrow[0:1, 0:1], start=True,
                                             stop=True).then_inc(h, 1))
            Rs(lambda En, h=bar.h: En.activation(wrk[0:1, 30:31], onef[0:1, 0:1],
                                                 AF.Copy).then_inc(h, 1))
            Rv(lambda En, h=bar.h: En.tensor_copy(wrk[0:1, 31:32],
                                                  onef[0:1, 1:2]).then_inc(h, 1))
            for R in (Rt, Rs, Rv):
                wait(R, bar, v)
            bar.n = v

        def VC(*fns, final=None):
            n = len(fns)
            res = None
            for k, fn in enumerate(fns):
                if k > 0:
                    wait(Rv, vv, vv.n)
                sem = (final if (final is not None and k == n - 1) else vv)
                res = sem.tick()
                Rv(lambda En, fn=fn, h=sem.h: fn(En).then_inc(h, 1))
            return res

        def dma(R, sem, dst, src, k=16):
            sem.tick(k)
            R(lambda En, d=dst, s=src, h=sem.h, k=k: En.dma_start(out=d, in_=s).then_inc(h, k))

        dma(Ry, dC, ident[:], id_d)
        dma(Ry, dC, onemat[:], om_d)
        dma(Ry, dC, onesrow[:], or_d)
        dma(Ry, dC, onef[:], of_d)
        dma(Ry, dC, nqw[:], nqw_d)
        dma(Ry, dC, sgc[:], sgc_d)
        dma(Ry, dC, suc[:], suc_d)
        dma(Ry, dC, beta_s[:], beta_d)
        dma(Rg, dG, wqT[:], wqT_d)
        dma(Rg, dG, wkT[:], wkT_d)
        dma(Rg, dG, wvT[:], wvT_d)
        dc_consts = dC.n
        dg_consts = dG.n

        wait(Rs, dC, dc_consts)
        wait(Rs, dG, dg_consts)
        t = cS.tick()
        Rs(lambda En: En.activation(bv1[:], beta_s[:], AF.Sigmoid).then_inc(cS.h, 1))
        wait(Rt, dC, dc_consts)
        wait(Rt, dG, dg_consts)
        wait(Rt, cS, t)
        t = cT.tick()
        Rt(lambda En: En.matmul(PS[:, 0:1], onef[0:1, 0:128], bv1[0:1, 0:1],
                                start=True, stop=True).then_inc(cT.h, 1))
        wait(Rv, cT, t)
        t = cV.tick()
        Rv(lambda En: En.tensor_copy(bvec[:], PS[:, 0:1]).then_inc(cV.h, 1))
        pre_win_ld = []
        pre_b_ld = []
        for i in range(2):
            dma(Ry, sW[i], ARf[:, (172032 + i * 8192) // 4:(172032 + i * 8192) // 4 + 2048],
                win_d[i])
            pre_win_ld.append(sW[i].n)
            dma(Ry, sA[i], ARf[:, (131072 + i * 8192) // 4:(131072 + i * 8192) // 4 + 2048],
                x_d[i])
            pre_b_ld.append(sA[i].n)
        barrier()

        nqT = bfv(O_NQT, 16384)
        nqT3 = nqT.rearrange("p (ec t) -> p ec t", t=T)
        W2T = bfv(O_W2T, 32768)
        W2T3 = W2T.rearrange("p (ec hd) -> p ec hd", hd=E)
        PF = PS[:, 2048:4096]

        nq_evac_val = [None] * NT
        nq_pass_val = [None] * NT
        tr_val = [None] * NT
        chain1_val = [None] * NT
        fold_mm_val = [None] * H
        fold_evac_val = [None] * H

        win_cv = [None] * H
        win_ld = [None] * H

        def win_load(h):
            winf = f32v(O_WINF[h % 2], 2048)
            if h >= 2:
                wait(Ry, cG, win_cv[h - 2])
            dma(Ry, sW[h % 2], winf, win_d[h])
            win_ld[h] = sW[h % 2].n

        def emit_fold(h):
            winb = bfv(O_WIN[h % 2], 2048)
            winf = f32v(O_WINF[h % 2], 2048)
            wait(Rg, sW[h % 2], win_ld[h])
            if h >= 2:
                wait(Rg, cT, fold_mm_val[h - 2])
            tcv = cG.tick()
            Rg(lambda En, winb=winb, winf=winf:
               En.tensor_copy(winb, winf).then_inc(cG.h, 1))
            win_cv[h] = tcv
            if h + 2 < H:
                win_load(h + 2)
            wait(Rt, cG, tcv)
            src3 = PF.rearrange("p (ec d) -> p ec d", d=128)
            dst3 = W2T3[:, :, h * 128:(h + 1) * 128]
            for hf in range(2):
                if h >= 1:
                    wait(Rt, cS, fold_evac_val[h - 1][hf])
                for ec8 in range(8):
                    ec = hf * 8 + ec8
                    t1 = cT.tick()
                    if ec8 == 7:
                        Rt(lambda En, winb=winb, ec=ec:
                           En.matmul(PF[:, ec * 128:(ec + 1) * 128],
                                     winb[:, ec * 128:(ec + 1) * 128],
                                     wqT[:], start=True, stop=True).then_inc(cT.h, 1))
                    else:
                        cT.n -= 1
                        Rt(lambda En, winb=winb, ec=ec:
                           En.matmul(PF[:, ec * 128:(ec + 1) * 128],
                                     winb[:, ec * 128:(ec + 1) * 128],
                                     wqT[:], start=True, stop=True))
                if hf == 1:
                    fold_mm_val[h] = t1
                wait(Rs, cT, t1)
                t2 = cS.tick()
                Rs(lambda En, d=dst3, s=src3, hf=hf:
                   En.activation(d[:, hf * 8:hf * 8 + 8, :],
                                 s[:, hf * 8:hf * 8 + 8, :], AF.Copy).then_inc(cS.h, 1))
                if fold_evac_val[h] is None:
                    fold_evac_val[h] = [None, None]
                fold_evac_val[h][hf] = t2

        def B_load(j):
            xt = f32v(O_XT[j % 3], 2048)
            if j >= 3:
                wait(Ry, cS, nq_pass_val[j - 3])
            dma(Ry, sA[j % 3], xt, x_d[j])
            b_ld_val[j] = sA[j % 3].n

        def B_V1(j):
            xt = f32v(O_XT[j % 3], 2048)
            W = 32 + j * 16
            ST = 176 + (j % 2) * 24
            wait(Rv, sA[j % 3], b_ld_val[j])
            for i in range(4):
                if i == 3:
                    t = vv.tick()
                    Rv(lambda En, i=i, xt=xt, ST=ST:
                       En.bn_stats(wrk[:, ST + i * 6:ST + (i + 1) * 6],
                                   xt[:, i * 512:(i + 1) * 512]).then_inc(vv.h, 1))
                else:
                    Rv(lambda En, i=i, xt=xt, ST=ST:
                       En.bn_stats(wrk[:, ST + i * 6:ST + (i + 1) * 6],
                                   xt[:, i * 512:(i + 1) * 512]))
            wait(Rv, vv, t)
            chain1_val[j] = VC(
                lambda En, W=W, ST=ST: En.bn_aggr(wrk[:, W + 3:W + 5], wrk[:, ST:ST + 24]),
                lambda En, W=W: En.tensor_scalar(wrk[:, W + 5:W + 6], wrk[:, W + 4:W + 5],
                                                 EPS_LN, None, ALU.add),
                lambda En, W=W: En.tensor_scalar(wrk[:, W + 6:W + 7], wrk[:, W + 4:W + 5],
                                                 1.0 + EPS_RMS, EPS_RMS * EPS_LN,
                                                 ALU.mult, ALU.add),
                lambda En, W=W: En.reciprocal(wrk[:, W + 7:W + 8], wrk[:, W + 5:W + 6]),
                lambda En, W=W: En.reciprocal(wrk[:, W + 8:W + 9], wrk[:, W + 6:W + 7]),
                lambda En, W=W: En.tensor_scalar(wrk[:, W + 13:W + 14], wrk[:, W + 3:W + 4],
                                                 -1.0, None, ALU.mult),
                final=cV)

        def B_S2(j):
            W = 32 + j * 16
            wait(Rs, cV, chain1_val[j])
            Rs(lambda En, W=W: En.sqrt(wrk[:, W + 9:W + 10], wrk[:, W + 7:W + 8]))
            t = ss.tick()
            Rs(lambda En, W=W: En.sqrt(wrk[:, W + 10:W + 11],
                                       wrk[:, W + 8:W + 9]).then_inc(ss.h, 1))
            b_sqrt_val[j] = t

        def B_V1b(j):
            W = 32 + j * 16
            wait(Rv, ss, b_sqrt_val[j])
            chain2_val[j] = VC(
                lambda En, W=W: En.tensor_mul(wrk[:, W + 11:W + 12], wrk[:, W + 10:W + 11],
                                              wrk[:, W + 13:W + 14]),
                lambda En, j=j, W=W: En.tensor_mul(bbs[:, j:j + 1], wrk[:, W + 9:W + 10],
                                                   wrk[:, W + 13:W + 14]),
                lambda En, j=j, W=W: En.tensor_scalar(a1s[:, j:j + 1], wrk[:, W + 9:W + 10],
                                                      1.0, None, ALU.add),
                final=cV)

        def B_S2b(j):
            xt = f32v(O_XT[j % 3], 2048)
            nqs = bfv(O_NQS[j % 2], 2048)
            W = 32 + j * 16
            wait(Rs, cV, chain2_val[j])
            if j >= 2:
                wait(Rs, cT, tr_val[j - 2])
            t4 = cS.tick()
            Rs(lambda En, nqs=nqs, xt=xt, W=W:
               En.activation(nqs, xt, AF.Identity, bias=wrk[:, W + 11:W + 12],
                             scale=wrk[:, W + 10:W + 11]).then_inc(cS.h, 1))
            nq_pass_val[j] = t4

        def B_T1(j):
            nqs = bfv(O_NQS[j % 2], 2048)
            wait(Rt, cS, nq_pass_val[j])
            if j >= 2:
                wait(Rt, cS, nq_evac_val[j - 2])
            ptb = PSbf[:, (j % 2) * 2048:(j % 2) * 2048 + 2048]
            for ec in range(EC):
                if ec == EC - 1:
                    t5 = cT.tick()
                    Rt(lambda En, ptb=ptb, nqs=nqs, ec=ec:
                       En.transpose(ptb[:, ec * 128:(ec + 1) * 128],
                                    nqs[:, ec * 128:(ec + 1) * 128], ident[:]).then_inc(cT.h, 1))
                else:
                    Rt(lambda En, ptb=ptb, nqs=nqs, ec=ec:
                       En.transpose(ptb[:, ec * 128:(ec + 1) * 128],
                                    nqs[:, ec * 128:(ec + 1) * 128], ident[:]))
            tr_val[j] = t5

        def B_S3(j):
            ptb = PSbf[:, (j % 2) * 2048:(j % 2) * 2048 + 2048]
            wait(Rs, cT, tr_val[j])
            t6 = cS.tick()
            src3 = ptb.rearrange("p (ec t) -> p ec t", t=128)
            dst3 = nqT3[:, :, j * 128:(j + 1) * 128]
            Rs(lambda En, d=dst3, s=src3: En.activation(d, s, AF.Copy).then_inc(cS.h, 1))
            nq_evac_val[j] = t6

        b_ld_val = [None] * NT
        b_sqrt_val = [None] * NT
        chain2_val = [None] * NT
        win_ld[0], win_ld[1] = pre_win_ld
        b_ld_val[0], b_ld_val[1] = pre_b_ld
        for k in range(NT + 2):
            if 0 <= k - 1 < NT:
                B_S2(k - 1)
                B_V1b(k - 1)
                B_S2b(k - 1)
                B_T1(k - 1)
            if 0 <= k - 2 < NT:
                B_S3(k - 2)
            if k < NT:
                emit_fold(2 * k)
                emit_fold(2 * k + 1)
                if k + 2 < NT:
                    B_load(k + 2)
                B_V1(k)
        barrier()

        qT = bfv(O_QT, 16384)
        q_evac_val = [None] * H
        nqw_val = [None] * EC
        for ec in range(EC):
            t = cS.tick()
            Rs(lambda En, ec=ec: En.activation(nqT3[:, ec, :], nqT3[:, ec, :], AF.Identity,
                                               scale=nqw[:, ec:ec + 1]).then_inc(cS.h, 1))
            nqw_val[ec] = t
        for hc in range(H):
            pq = PS[:, (hc % 4) * 1024:(hc % 4) * 1024 + 1024]
            if hc == 1:
                wait(Rt, cS, nqw_val[EC - 1])
            if hc >= 4:
                wait(Rt, cS, q_evac_val[hc - 4])
            for half in range(2):
                for ec in range(EC):
                    if hc == 0 and half == 0:
                        wait(Rt, cS, nqw_val[ec])
                    lastmm = half == 1 and ec == EC - 1
                    if lastmm:
                        t1 = cT.tick()
                        Rt(lambda En, pq=pq, ec=ec, half=half, hc=hc:
                           En.matmul(pq[:, half * 512:half * 512 + 512],
                                     W2T3[:, ec, hc * 128:(hc + 1) * 128],
                                     nqT3[:, ec, half * 512:half * 512 + 512],
                                     start=(ec == 0), stop=(ec == EC - 1)).then_inc(cT.h, 1))
                    else:
                        Rt(lambda En, pq=pq, ec=ec, half=half, hc=hc:
                           En.matmul(pq[:, half * 512:half * 512 + 512],
                                     W2T3[:, ec, hc * 128:(hc + 1) * 128],
                                     nqT3[:, ec, half * 512:half * 512 + 512],
                                     start=(ec == 0), stop=(ec == EC - 1)))
            wait(Rs, cT, t1)
            t2 = cS.tick()
            Rs(lambda En, hc=hc, pq=pq:
               En.activation(qT[:, hc * 1024:(hc + 1) * 1024], pq, AF.Identity,
                             scale=bvec[:]).then_inc(cS.h, 1))
            q_evac_val[hc] = t2
        barrier()

        kT = bfv(O_KT, 16384)
        vsb = bfv(O_V, 16384)
        wait(Ry, bar, bar.n)
        e_mm_val = [None] * H
        e_kevac_val = [None] * H
        e_evac_val = [None] * H
        pat_cv = [None] * H
        for h in range(H):
            pat = bfv(O_PAT[h % 2], 1024)
            patf = f32v(O_PATF[h % 2], 1024)
            PK = PS[:, (h % 2) * 1024:(h % 2) * 1024 + 1024]
            PV = PS[:, 2048 + (h % 2) * 1024:2048 + (h % 2) * 1024 + 1024]
            if h >= 2:
                wait(Ry, cG, pat_cv[h - 2])
            dma(Ry, sW[h % 2], patf, patT_d[h])
            ldv = sW[h % 2].n
            wait(Rg, sW[h % 2], ldv)
            if h >= 2:
                wait(Rg, cT, e_mm_val[h - 2])
            tcv = cG.tick()
            Rg(lambda En, pat=pat, patf=patf:
               En.tensor_copy(pat, patf).then_inc(cG.h, 1))
            pat_cv[h] = tcv
            wait(Rt, cG, tcv)
            if h >= 2:
                wait(Rt, cS, e_kevac_val[h - 2])
            for half in range(2):
                if half == 1:
                    tk = cT.tick()
                    Rt(lambda En, pat=pat, half=half, PK=PK:
                       En.matmul(PK[:, half * 512:half * 512 + 512], wkT[:],
                                 pat[:, half * 512:half * 512 + 512], start=True,
                                 stop=True).then_inc(cT.h, 1))
                else:
                    Rt(lambda En, pat=pat, half=half, PK=PK:
                       En.matmul(PK[:, half * 512:half * 512 + 512], wkT[:],
                                 pat[:, half * 512:half * 512 + 512], start=True, stop=True))
            if h >= 2:
                wait(Rt, cS, e_evac_val[h - 2])
            for mc in range(MC):
                if mc == MC - 1:
                    t1 = cT.tick()
                    Rt(lambda En, pat=pat, mc=mc, PV=PV:
                       En.matmul(PV[:, mc * 128:(mc + 1) * 128], pat[:, mc * 128:(mc + 1) * 128],
                                 wvT[:], start=True, stop=True).then_inc(cT.h, 1))
                else:
                    Rt(lambda En, pat=pat, mc=mc, PV=PV:
                       En.matmul(PV[:, mc * 128:(mc + 1) * 128], pat[:, mc * 128:(mc + 1) * 128],
                                 wvT[:], start=True, stop=True))
            e_mm_val[h] = t1
            wait(Rs, cT, tk)
            t2 = cS.tick()
            Rs(lambda En, h=h, PK=PK: En.activation(kT[:, h * 1024:(h + 1) * 1024], PK,
                                                    AF.Copy).then_inc(cS.h, 1))
            e_kevac_val[h] = t2
            wait(Rs, cT, t1)
            t3 = cS.tick()
            Rs(lambda En, h=h, PV=PV: En.activation(vsb[:, h * 1024:(h + 1) * 1024], PV,
                                                    AF.Copy).then_inc(cS.h, 1))
            e_evac_val[h] = t3
        barrier()

        Abf = bfv(O_A, 16384)
        A3 = Abf.rearrange("p (j e) -> p j e", e=E)
        eT = bfv(O_ET, 8192)
        PU = PS[:, 2048:3072]
        PSS = PS[:, 3072:4096]
        pTb = PSbf[:, 7 * 1024:8 * 1024]
        exp_vals = []
        uevac_val = [None] * H
        aevac_val = [None] * H
        recip_val = [None] * H
        for h in range(H):
            usb = bfv(O_U[h % 2], 1024)

            def sumU(m, h=h):
                if m == 0 and h >= 1:
                    wait(Rt, cS, aevac_val[h - 1])
                    wait(Rt, cV, recip_val[h - 1])
                wait(Rt, cS, exp_vals[h * MC + m])
                for half in range(2):
                    Rt(lambda En, m=m, half=half:
                       En.matmul(PSS[:, half * 512:half * 512 + 512], onemat[:],
                                 eT[:, m * 1024 + half * 512:m * 1024 + half * 512 + 512],
                                 start=(m == 0), stop=(m == MC - 1)))
                t3 = None
                for half in range(2):
                    lastmm = m == MC - 1 and half == 1
                    if lastmm:
                        t3 = cT.tick()
                        Rt(lambda En, h=h, m=m, half=half:
                           En.matmul(PU[:, half * 512:half * 512 + 512],
                                     vsb[:, (h * MC + m) * 128:(h * MC + m + 1) * 128],
                                     eT[:, m * 1024 + half * 512:m * 1024 + half * 512 + 512],
                                     start=(m == 0), stop=(m == MC - 1)).then_inc(cT.h, 1))
                    else:
                        Rt(lambda En, h=h, m=m, half=half:
                           En.matmul(PU[:, half * 512:half * 512 + 512],
                                     vsb[:, (h * MC + m) * 128:(h * MC + m + 1) * 128],
                                     eT[:, m * 1024 + half * 512:m * 1024 + half * 512 + 512],
                                     start=(m == 0), stop=(m == MC - 1)))
                return t3

            for mc in range(MC):
                K = h * MC + mc
                p = K % 2
                psb = PS[:, p * 1024:p * 1024 + 1024]
                if K >= 2:
                    wait(Rt, cS, exp_vals[K - 2])
                for half in range(2):
                    if half == 1:
                        t1 = cT.tick()
                        Rt(lambda En, psb=psb, h=h, mc=mc, half=half:
                           En.matmul(psb[:, half * 512:half * 512 + 512],
                                     kT[:, h * 1024 + mc * 128:h * 1024 + (mc + 1) * 128],
                                     qT[:, h * 1024 + half * 512:h * 1024 + half * 512 + 512],
                                     start=True, stop=True).then_inc(cT.h, 1))
                    else:
                        Rt(lambda En, psb=psb, h=h, mc=mc, half=half:
                           En.matmul(psb[:, half * 512:half * 512 + 512],
                                     kT[:, h * 1024 + mc * 128:h * 1024 + (mc + 1) * 128],
                                     qT[:, h * 1024 + half * 512:h * 1024 + half * 512 + 512],
                                     start=True, stop=True))
                wait(Rs, cT, t1)
                t2 = cS.tick()
                Rs(lambda En, mc=mc, psb=psb:
                   En.activation(eT[:, mc * 1024:(mc + 1) * 1024], psb, AF.Exp).then_inc(cS.h, 1))
                exp_vals.append(t2)
                if mc >= 1:
                    sumU(mc - 1)
            t3 = sumU(MC - 1)
            wait(Rs, cT, t3)
            tS = cS.tick()
            Rs(lambda En: En.activation(srow[:], PSS[0:1, :], AF.Copy).then_inc(cS.h, 1))
            t5 = cS.tick()
            Rs(lambda En, usb=usb: En.activation(usb, PU, AF.Copy).then_inc(cS.h, 1))
            uevac_val[h] = t5
            wait(Rt, cS, tS)
            for j in range(NT):
                if j == NT - 1:
                    t6 = cT.tick()
                    Rt(lambda En, j=j: En.matmul(PS[:, 3072 + j:3072 + j + 1],
                                                 srow[0:1, j * 128:(j + 1) * 128],
                                                 onef[0:1, 0:1], start=True,
                                                 stop=True).then_inc(cT.h, 1))
                else:
                    Rt(lambda En, j=j: En.matmul(PS[:, 3072 + j:3072 + j + 1],
                                                 srow[0:1, j * 128:(j + 1) * 128],
                                                 onef[0:1, 0:1], start=True, stop=True))
            wait(Rv, cT, t6)
            t7 = cV.tick()
            Rv(lambda En, h=h: En.reciprocal(rscol[:, h * NT:h * NT + NT],
                                             PS[:, 3072:3072 + NT]).then_inc(cV.h, 1))
            recip_val[h] = t7
            wait(Rt, cS, t5)
            for j in range(NT):
                if j == NT - 1:
                    t8 = cT.tick()
                    Rt(lambda En, usb=usb, j=j:
                       En.transpose(pTb[:, j * 128:(j + 1) * 128], usb[:, j * 128:(j + 1) * 128],
                                    ident[:]).then_inc(cT.h, 1))
                else:
                    Rt(lambda En, usb=usb, j=j:
                       En.transpose(pTb[:, j * 128:(j + 1) * 128], usb[:, j * 128:(j + 1) * 128],
                                    ident[:]))
            wait(Rs, cT, t8)
            t9 = cS.tick()
            src3 = pTb.rearrange("p (j d) -> p j d", d=128)
            dst3 = A3[:, :, h * 128:(h + 1) * 128]
            Rs(lambda En, d=dst3, s=src3: En.activation(d, s, AF.Copy).then_inc(cS.h, 1))
            aevac_val[h] = t9
        f_t_last = t8

        x1nT = bfv(O_X1NT, 16384)
        x1nT3 = x1nT.rearrange("p (hc t) -> p hc t", t=T)
        wrrep = f32v(O_WRREP, 2048)
        dma(Ry, dC, wrrep, wr_d)
        dc_wr = dC.n
        g_ld = [None] * NT
        p1val = [None] * NT
        p1sval = [None] * NT
        sq1val = [None] * NT
        v2val = [None] * NT
        rsqv = [None] * NT
        xaval = [None] * NT
        t3val = [None] * NT
        adv_v = [None] * NT
        adv_g = [None] * NT
        dxval = [None] * NT
        bnval = [None] * NT
        x1nval = [None] * NT
        tr2_val = [None] * NT
        g_evac_val = [None] * NT
        sqd = f32v(O_SQG[0], 2048)

        def G_load(j):
            xre = f32v(O_XRE[j % 2], 2048)
            if j < 2:
                wait(Ry, cT, f_t_last)
            if j >= 2:
                wait(Ry, cS, xaval[j - 2])
            dma(Ry, sA[j % 2], xre, x_d[j])
            g_ld[j] = sA[j % 2].n

        def G_xar(j):
            xre = f32v(O_XRE[j % 2], 2048)
            xar = f32v(O_XA[j % 2], 2048)
            wait(Rs, sA[j % 2], g_ld[j])
            if j < 2:
                wait(Rs, cT, f_t_last)
            if j >= 2:
                wait(Rs, cG, adv_g[j - 2])
                wait(Rs, cV, adv_v[j - 2])
            t = cS.tick()
            Rs(lambda En, xar=xar, xre=xre, j=j:
               En.activation(xar, xre, AF.Identity, scale=a1s[:, j:j + 1],
                             bias=bbs[:, j:j + 1]).then_inc(cS.h, 1))
            xaval[j] = t

        def G_P1(j):
            t0 = f32v(O_T0[j % 2], 2048)
            if j < 2:
                wait(Rv, cT, f_t_last)
                wait(Rv, cS, aevac_val[H - 1])
                wait(Rv, cV, recip_val[H - 1])
            if j >= 2:
                wait(Rv, cS, sq1val[j - 2])
            for hh in range(4, H):
                if hh == H - 1:
                    t = cV.tick()
                    Rv(lambda En, j=j, hh=hh, t0=t0:
                       En.tensor_scalar(t0[:, hh * 128:(hh + 1) * 128],
                                        A3[:, j, hh * 128:(hh + 1) * 128],
                                        rscol[:, hh * NT + j:hh * NT + j + 1], None,
                                        ALU.mult).then_inc(cV.h, 1))
                else:
                    Rv(lambda En, j=j, hh=hh, t0=t0:
                       En.tensor_scalar(t0[:, hh * 128:(hh + 1) * 128],
                                        A3[:, j, hh * 128:(hh + 1) * 128],
                                        rscol[:, hh * NT + j:hh * NT + j + 1], None,
                                        ALU.mult))
            p1val[j] = t

        def G_P1s(j):
            t0 = f32v(O_T0[j % 2], 2048)
            if j < 2:
                wait(Rs, cV, recip_val[H - 1])
            if j >= 2:
                wait(Rs, cV, t3val[j - 2])
            for hh in range(4):
                if hh == 3:
                    t = ss.tick()
                    Rs(lambda En, j=j, hh=hh, t0=t0:
                       En.activation(t0[:, hh * 128:(hh + 1) * 128],
                                     A3[:, j, hh * 128:(hh + 1) * 128], AF.Identity,
                                     scale=rscol[:, hh * NT + j:hh * NT + j + 1]).then_inc(ss.h, 1))
                    p1sval[j] = t
                else:
                    Rs(lambda En, j=j, hh=hh, t0=t0:
                       En.activation(t0[:, hh * 128:(hh + 1) * 128],
                                     A3[:, j, hh * 128:(hh + 1) * 128], AF.Identity,
                                     scale=rscol[:, hh * NT + j:hh * NT + j + 1]))

        def G_S1(j):
            t0 = f32v(O_T0[j % 2], 2048)
            W = 32 + j * 16
            wait(Rs, cV, p1val[j])
            wait(Rs, ss, p1sval[j])
            t = cS.tick()
            Rs(lambda En, t0=t0, W=W:
               En.activation(sqd, t0, AF.Square,
                             accum_out=wrk[:, W:W + 1]).then_inc(cS.h, 1))
            sq1val[j] = t

        def G_V2(j):
            W = 32 + j * 16
            wait(Rv, cS, sq1val[j])
            v2val[j] = VC(
                lambda En, W=W: En.tensor_scalar(wrk[:, W + 1:W + 2], wrk[:, W:W + 1],
                                                 1.0 / E, EPS_RMS, ALU.mult, ALU.add),
                lambda En, W=W: En.reciprocal(wrk[:, W + 2:W + 3], wrk[:, W + 1:W + 2]),
                final=cV)

        def G_sqrt(j):
            W = 32 + j * 16
            wait(Rs, cV, v2val[j])
            t = ss.tick()
            Rs(lambda En, W=W: En.sqrt(wrk[:, W + 3:W + 4], wrk[:, W + 2:W + 3]).then_inc(ss.h, 1))
            rsqv[j] = t

        def G_T3(j):
            t0 = f32v(O_T0[j % 2], 2048)
            t3r = f32v(O_T3[j % 2], 2048)
            W = 32 + j * 16
            wait(Rv, ss, rsqv[j])
            wait(Rv, dC, dc_wr)
            if j < 2:
                wait(Rv, cT, f_t_last)
            if j >= 2:
                wait(Rv, cG, adv_g[j - 2])
            t = cV.tick()
            Rv(lambda En, t3r=t3r, t0=t0, W=W:
               En.scalar_tensor_tensor(t3r, t0, wrk[:, W + 3:W + 4], wrrep,
                                       ALU.mult, ALU.mult).then_inc(cV.h, 1))
            t3val[j] = t

        def G_AD(j):
            t3r = f32v(O_T3[j % 2], 2048)
            xar = f32v(O_XA[j % 2], 2048)
            x1t = f32v(O_X1[j % 2], 2048)
            if j >= 2:
                wait(Rv, cS, x1nval[j - 2])
                wait(Rv, dX[j % 2], dxval[j - 2])
            wait(Rv, cV, t3val[j])
            t = cV.tick()
            Rv(lambda En, x1t=x1t, t3r=t3r, xar=xar:
               En.tensor_add(x1t[:, 0:1024], t3r[:, 0:1024],
                             xar[:, 0:1024]).then_inc(cV.h, 1))
            adv_v[j] = t
            wait(Rg, cV, t3val[j])
            wait(Rg, cS, xaval[j])
            if j < 2:
                wait(Rg, cT, f_t_last)
            if j >= 2:
                wait(Rg, cV, bnval[j - 2])
                wait(Rg, cS, x1nval[j - 2])
                wait(Rg, dX[j % 2], dxval[j - 2])
            t = cG.tick()
            Rg(lambda En, x1t=x1t, t3r=t3r, xar=xar:
               En.tensor_add(x1t[:, 1024:2048], t3r[:, 1024:2048],
                             xar[:, 1024:2048]).then_inc(cG.h, 1))
            adv_g[j] = t

        def G_SP(j):
            x1t = f32v(O_X1[j % 2], 2048)
            wait(Ry, cV, adv_v[j])
            wait(Ry, cG, adv_g[j])
            dma(Ry, dX[j % 2], x1sp_d[j], x1t)
            dxval[j] = dX[j % 2].n

        def G_BN(j):
            x1t = f32v(O_X1[j % 2], 2048)
            W = 32 + j * 16
            ST = 176 + (j % 2) * 24
            wait(Rv, cG, adv_g[j])
            wait(Rv, cV, adv_v[j])
            for i in range(4):
                if i == 3:
                    t = vv.tick()
                    Rv(lambda En, i=i, x1t=x1t, ST=ST:
                       En.bn_stats(wrk[:, ST + i * 6:ST + (i + 1) * 6],
                                   x1t[:, i * 512:(i + 1) * 512]).then_inc(vv.h, 1))
                else:
                    Rv(lambda En, i=i, x1t=x1t, ST=ST:
                       En.bn_stats(wrk[:, ST + i * 6:ST + (i + 1) * 6],
                                   x1t[:, i * 512:(i + 1) * 512]))
            wait(Rv, vv, t)
            bnval[j] = VC(
                lambda En, W=W, ST=ST: En.bn_aggr(wrk[:, W + 6:W + 8], wrk[:, ST:ST + 24]),
                lambda En, W=W: En.tensor_scalar(wrk[:, W + 8:W + 9], wrk[:, W + 7:W + 8],
                                                 EPS_LN, None, ALU.add),
                lambda En, W=W: En.reciprocal(wrk[:, W + 9:W + 10], wrk[:, W + 8:W + 9]),
                final=cV)

        def G_SQ(j):
            x1t = f32v(O_X1[j % 2], 2048)
            x1n = bfv(O_X1N[j % 2], 2048)
            W = 32 + j * 16
            wait(Rs, cV, bnval[j])
            t = ss.tick()
            Rs(lambda En, W=W: En.sqrt(wrk[:, W + 10:W + 11],
                                       wrk[:, W + 9:W + 10]).then_inc(ss.h, 1))
            wait(Rs, ss, t)
            t = ss.tick()
            Rs(lambda En, W=W: En.mul(wrk[:, W + 12:W + 13], wrk[:, W + 10:W + 11],
                                      -1.0).then_inc(ss.h, 1))
            wait(Rs, ss, t)
            t = ss.tick()
            Rs(lambda En, W=W: En.mul(wrk[:, W + 11:W + 12], wrk[:, W + 12:W + 13],
                                      wrk[:, W + 6:W + 7]).then_inc(ss.h, 1))
            wait(Rs, ss, t)
            if j >= 2:
                wait(Rs, cT, tr2_val[j - 2])
            wait(Rs, cV, adv_v[j])
            t = cS.tick()
            Rs(lambda En, x1n=x1n, x1t=x1t, W=W:
               En.activation(x1n, x1t, AF.Identity, scale=wrk[:, W + 10:W + 11],
                             bias=wrk[:, W + 11:W + 12]).then_inc(cS.h, 1))
            x1nval[j] = t

        def G_T1(j):
            x1n = bfv(O_X1N[j % 2], 2048)
            wait(Rt, cS, x1nval[j])
            if j >= 2:
                wait(Rt, cV, g_evac_val[j - 2])
            ptb = PSbf[:, (j % 2) * 2048:(j % 2) * 2048 + 2048]
            for ec in range(EC):
                if ec == EC - 1:
                    tH = cT.tick()
                    Rt(lambda En, ptb=ptb, x1n=x1n, ec=ec:
                       En.transpose(ptb[:, ec * 128:(ec + 1) * 128],
                                    x1n[:, ec * 128:(ec + 1) * 128], ident[:]).then_inc(cT.h, 1))
                else:
                    Rt(lambda En, ptb=ptb, x1n=x1n, ec=ec:
                       En.transpose(ptb[:, ec * 128:(ec + 1) * 128],
                                    x1n[:, ec * 128:(ec + 1) * 128], ident[:]))
            tr2_val[j] = tH

        def G_SE(j):
            ptb = PSbf[:, (j % 2) * 2048:(j % 2) * 2048 + 2048]
            wait(Rv, cT, tr2_val[j])
            tI = cV.tick()
            src3 = ptb.rearrange("p (ec t) -> p ec t", t=128)
            dst3 = x1nT3[:, :, j * 128:(j + 1) * 128]
            Rv(lambda En, d=dst3, s=src3: En.tensor_copy(d, s).then_inc(cV.h, 1))
            g_evac_val[j] = tI

        G_load(0)
        G_load(1)
        for k in range(NT + 2):
            if 0 <= k - 1 < NT:
                G_V2(k - 1)
                G_sqrt(k - 1)
            if 0 <= k - 2 < NT:
                G_BN(k - 2)
                G_SQ(k - 2)
                G_T1(k - 2)
            if k < NT:
                G_xar(k)
                G_P1(k)
                G_P1s(k)
                G_S1(k)
                if k + 2 < NT:
                    G_load(k + 2)
            if 0 <= k - 1 < NT:
                G_T3(k - 1)
                G_AD(k - 1)
                G_SP(k - 1)
            if 0 <= k - 2 < NT:
                G_SE(k - 2)
        barrier()

        fuse = bfv(O_FUSE, 65536)
        fuse3 = fuse.rearrange("p (ic t) -> p ic t", t=T)
        wait(Rv, dX[0], dX[0].n)
        wait(Rv, dX[1], dX[1].n)
        h_mm_val = [None] * IC
        h_sg_val = [None] * IC
        h_fuse_val = [None] * IC
        wait(Rg, bar, bar.n)
        for ic in range(IC):
            wgb = bfv(O_WG[ic % 2], 2048)
            wub = bfv(O_WU[ic % 2], 2048)
            sgb = bfv(O_SG[ic % 2], 1024)
            sub = bfv(O_SU[ic % 2], 1024)
            if ic >= 2:
                wait(Rg, cT, h_mm_val[ic - 2])
            dma(Rg, sB[ic % 2], wgb, wg_d[ic])
            ldg = sB[ic % 2].n
            dma(Rg, sCm[ic % 2], wub, wu_d[ic])
            ldu = sCm[ic % 2].n
            p = ic % 2
            PG = PS[:, p * 1024:p * 1024 + 1024]
            PU2 = PS[:, 2048 + p * 1024:2048 + p * 1024 + 1024]
            wait(Rt, sB[ic % 2], ldg)
            wait(Rt, sCm[ic % 2], ldu)
            if ic >= 2:
                wait(Rt, cS, h_sg_val[ic - 2])
            for half in range(2):
                for hc in range(EC):
                    Rt(lambda En, PG=PG, wgb=wgb, hc=hc, half=half:
                       En.matmul(PG[:, half * 512:half * 512 + 512], wgb[:, hc * 128:(hc + 1) * 128],
                                 x1nT3[:, hc, half * 512:half * 512 + 512],
                                 start=(hc == 0), stop=(hc == EC - 1)))
            for half in range(2):
                for hc in range(EC):
                    lastmm = half == 1 and hc == EC - 1
                    if lastmm:
                        t1 = cT.tick()
                        Rt(lambda En, PU2=PU2, wub=wub, hc=hc, half=half:
                           En.matmul(PU2[:, half * 512:half * 512 + 512],
                                     wub[:, hc * 128:(hc + 1) * 128],
                                     x1nT3[:, hc, half * 512:half * 512 + 512],
                                     start=(hc == 0), stop=(hc == EC - 1)).then_inc(cT.h, 1))
                    else:
                        Rt(lambda En, PU2=PU2, wub=wub, hc=hc, half=half:
                           En.matmul(PU2[:, half * 512:half * 512 + 512],
                                     wub[:, hc * 128:(hc + 1) * 128],
                                     x1nT3[:, hc, half * 512:half * 512 + 512],
                                     start=(hc == 0), stop=(hc == EC - 1)))
            h_mm_val[ic] = t1
            wait(Rs, cT, t1)
            if ic >= 2:
                wait(Rs, cV, h_fuse_val[ic - 2])
            Rs(lambda En, sgb=sgb, PG=PG, ic=ic:
               En.activation(sgb, PG, AF.Silu, scale=sgc[:, ic:ic + 1]))
            t2 = cS.tick()
            Rs(lambda En, sub=sub, PU2=PU2, ic=ic:
               En.activation(sub, PU2, AF.Identity, scale=suc[:, ic:ic + 1]).then_inc(cS.h, 1))
            h_sg_val[ic] = t2
            wait(Rv, cS, t2)
            t3 = cV.tick()
            Rv(lambda En, ic=ic, sgb=sgb, sub=sub:
               En.tensor_mul(fuse3[:, ic, :], sgb, sub).then_inc(cV.h, 1))
            h_fuse_val[ic] = t3
        barrier()

        sdrep = f32v(O_SDREP, 2048)
        wait(Ry, bar, bar.n)
        dma(Ry, dC, sdrep, sd_d)
        dc_sd = dC.n
        store_i = 0
        out_done = []
        i_mm_val = {}
        ep_j = [[None] * NT for _ in range(HQ)]
        wait(Rg, bar, bar.n)
        wait(Ry, dX[0], dX[0].n)
        wait(Ry, dX[1], dX[1].n)
        wd_cv = {}
        ep_list = []
        bank_done = [[None] * NT for _ in range(HQ)]
        store_ic = [0]

        def I_x1re_load(q, j):
            x1re = f32v(O_X1RE[j % 4], 512)
            idx = q * NT + j
            if idx >= 4:
                wait(Rg, cV, ep_list[idx - 4])
            dma(Rg, sXr[j % 4], x1re, x1sp_d[j][:, q * 512:(q + 1) * 512])
            x1re_ld[idx] = sXr[j % 4].n

        def I_epilogue(q, j):
            store_i = store_ic[0]
            x1re = f32v(O_X1RE[j % 4], 512)
            tmp = f32v(O_TMP[store_i % 2], 512)
            outb = f32v(O_OUT[store_i % 2], 512)
            idx = q * NT + j
            wait(Rv, cT, bank_done[q][j])
            wait(Rv, sXr[j % 4], x1re_ld[idx])
            wait(Rv, dC, dc_sd)
            if store_i >= 2:
                wait(Rv, cV, out_done[store_i - 2])
                wait(Rv, dSo[store_i % 2], dSo[store_i % 2].n)
            t = vv.tick()
            Rv(lambda En, j=j, q=q, tmp=tmp:
               En.tensor_mul(tmp, PS[:, j * 512:(j + 1) * 512],
                             sdrep[:, q * 512:(q + 1) * 512]).then_inc(vv.h, 1))
            wait(Rv, vv, t)
            t2 = cV.tick()
            Rv(lambda En, outb=outb, tmp=tmp, x1re=x1re:
               En.tensor_add(outb, tmp, x1re).then_inc(cV.h, 1))
            out_done.append(t2)
            ep_list.append(t2)
            ep_j[q][j] = t2
            wait(Rg, cV, t2)
            dma(Rg, dSo[store_i % 2], out_d[j][:, q * 512:(q + 1) * 512], outb)
            store_ic[0] += 1

        def I_load(gi):
            q, g = divmod(gi, 16)
            i8s = gi % 3
            wd8 = AR.bitcast(I8)[:, O_WD8[i8s]:O_WD8[i8s] + 2048]
            if gi >= 3:
                wait(Ry, cS, wd_cv[gi - 3])
            dma(Ry, sW[i8s], wd8, wd_d[q, g])
            wd_ld[gi] = sW[i8s].n

        def I_convert(gi):
            i8s = gi % 3
            bfs = gi % 6
            wd8 = AR.bitcast(I8)[:, O_WD8[i8s]:O_WD8[i8s] + 2048]
            wdb = bfv(O_WD[bfs], 2048)
            wait(Rs, sW[i8s], wd_ld[gi])
            if gi >= 6:
                pq_, pg_ = divmod(gi - 6, 16)
                wait(Rs, cT, i_mm_val[(pq_, pg_)])
            tcv = cS.tick()
            Rs(lambda En, wdb=wdb, wd8=wd8:
               En.activation(wdb, wd8, AF.Copy).then_inc(cS.h, 1))
            wd_cv[gi] = tcv

        def I_group(gi):
            q, g = divmod(gi, 16)
            bfs = gi % 6
            wdb = bfv(O_WD[bfs], 2048)
            wait(Rt, cS, wd_cv[gi])
            if g == 15:
                for j in range(4):
                    I_x1re_load(q, j)
                order = [(c, j) for j in range(NT) for c in range(4)]
            else:
                order = [(c, j) for c in range(4) for j in range(NT)]
            for (c, j) in order:
                ic = g * 4 + c
                if q > 0 and ic == 0:
                    wait(Rt, cV, ep_j[q - 1][j])
                tick = (ic == IC - 1) or (g < 15 and c == 3 and j == NT - 1)
                if tick:
                    t1 = cT.tick()
                    Rt(lambda En, ic=ic, j=j, c=c, wdb=wdb:
                       En.matmul(PS[:, j * 512:(j + 1) * 512],
                                 fuse3[:, ic, j * 128:(j + 1) * 128],
                                 wdb[:, c * 512:(c + 1) * 512],
                                 start=(ic == 0), stop=(ic == IC - 1)).then_inc(cT.h, 1))
                    if ic == IC - 1:
                        bank_done[q][j] = t1
                        I_epilogue(q, j)
                        if j + 4 < NT:
                            I_x1re_load(q, j + 4)
                else:
                    Rt(lambda En, ic=ic, j=j, c=c, wdb=wdb:
                       En.matmul(PS[:, j * 512:(j + 1) * 512],
                                 fuse3[:, ic, j * 128:(j + 1) * 128],
                                 wdb[:, c * 512:(c + 1) * 512],
                                 start=(ic == 0), stop=(ic == IC - 1)))
            i_mm_val[(q, g)] = cT.n

        wd_ld = {}
        x1re_ld = {}
        for gi in range(HQ * 16):
            I_load(gi)
            I_convert(gi)
            if gi >= 2:
                I_group(gi - 2)
        I_group(HQ * 16 - 2)
        I_group(HQ * 16 - 1)
        wait(Ry, dX[0], dX[0].n)
        wait(Ry, dX[1], dX[1].n)
        wait(Ry, dSo[0], dSo[0].n)
        wait(Ry, dSo[1], dSo[1].n)

        blk = ctx.enter_context(nc.Block())
        blk.tensor(lambda En: [f(En) for f in ops["t"]])
        blk.scalar(lambda En: [f(En) for f in ops["s"]])
        blk.vector(lambda En: [f(En) for f in ops["v"]])
        blk.sync(lambda En: [f(En) for f in ops["y"]])
        blk.gpsimd(lambda En: [f(En) for f in ops["g"]])
    return nc


_NC_CACHE = [None]


def _get_nc():
    if _NC_CACHE[0] is None:
        _NC_CACHE[0] = _build_nc()
    return _NC_CACHE[0]


def _prepare(L0_attn_query_input, L0_attn_W_in, L0_attn_W_q, L0_attn_W_k, L0_attn_W_v,
             L0_attn_stored_patterns, L0_attn_norm_query_w, L0_attn_norm_retrieved_w,
             L0_attn_beta, L0_ffn_w_gate, L0_ffn_w_up, L0_ffn_w_down, L0_ffn_s_gate,
             L0_ffn_s_up, L0_ffn_s_down):
    x = np.ascontiguousarray(np.asarray(L0_attn_query_input, dtype=np.float32))
    B, S, E_ = x.shape
    xt = x.reshape(B * S, E_)

    w_in = np.ascontiguousarray(np.asarray(L0_attn_W_in, dtype=np.float32)).reshape(H, 128, E)
    wqT = np.ascontiguousarray(np.asarray(L0_attn_W_q, dtype=np.float32).T)
    wkT = np.ascontiguousarray(np.asarray(L0_attn_W_k, dtype=np.float32).T)
    wvT = np.ascontiguousarray(np.asarray(L0_attn_W_v, dtype=np.float32).T)
    patT = np.ascontiguousarray(np.asarray(L0_attn_stored_patterns, dtype=np.float32).transpose(0, 2, 1))
    nqw = np.ascontiguousarray(np.asarray(L0_attn_norm_query_w, dtype=np.float32).reshape(EC, 128).T)
    wr = np.ascontiguousarray(np.broadcast_to(np.asarray(L0_attn_norm_retrieved_w, dtype=np.float32),
                                              (128, E)))
    sd = np.ascontiguousarray(np.broadcast_to(np.asarray(L0_ffn_s_down, dtype=np.float32), (128, E)))
    sgc = np.ascontiguousarray(np.asarray(L0_ffn_s_gate, dtype=np.float32).reshape(IC, 128).T)
    suc = np.ascontiguousarray(np.asarray(L0_ffn_s_up, dtype=np.float32).reshape(IC, 128).T)
    beta = np.asarray(L0_attn_beta, dtype=np.float32).reshape(1, 1)
    wg8 = np.asarray(L0_ffn_w_gate, dtype=np.int8)
    wu8 = np.asarray(L0_ffn_w_up, dtype=np.int8)
    wd8 = np.asarray(L0_ffn_w_down, dtype=np.int8)
    wg_blk = np.ascontiguousarray(wg8.reshape(EC, 128, IC, 128).transpose(2, 1, 0, 3)).reshape(IC, 128, E)
    wu_blk = np.ascontiguousarray(wu8.reshape(EC, 128, IC, 128).transpose(2, 1, 0, 3)).reshape(IC, 128, E)
    wd_blk = np.ascontiguousarray(wd8.reshape(16, 4, 128, HQ, 512).transpose(3, 0, 2, 1, 4)).reshape(
        HQ, 16, 128, E)

    ident = np.eye(128, dtype=ml_dtypes.bfloat16)
    onemat = np.ones((128, 128), dtype=ml_dtypes.bfloat16)
    onesrow = np.ones((1, 128), dtype=ml_dtypes.bfloat16)
    onef = np.ones((1, 128), dtype=np.float32)

    shared = dict(win=w_in, wqT=wqT, wkT=wkT, wvT=wvT, patT=patT, nqw=nqw, wr=wr, sd=sd,
                  sgc=sgc, suc=suc, beta=beta, wg=wg_blk, wu=wu_blk, wd=wd_blk,
                  ident=ident, onemat=onemat, onesrow=onesrow, onef=onef)
    in_maps = []
    for c in range(NCORE):
        xs = np.ascontiguousarray(xt[c * T:(c + 1) * T].reshape(NT, 128, E))
        in_maps.append(dict(shared, x=xs))
    return in_maps, (B, S, E_)


def kernel(**inputs):
    in_maps, (B, S, E_) = _prepare(**inputs)
    nc = _get_nc()
    res = run_bass_kernel_spmd(nc, in_maps, core_ids=list(range(NCORE)))
    outs = [np.asarray(res.results[c]["out"], dtype=np.float32).reshape(T, E_) for c in range(NCORE)]
    return np.concatenate(outs, axis=0).reshape(B, S, E_)
